# version 09
# baseline (speedup 1.0000x reference)
import contextlib
import numpy as np
import ml_dtypes
import concourse.bass as bass
import concourse.mybir as mybir
from concourse.bass_utils import run_bass_kernel_spmd

F32 = mybir.dt.float32
BF16 = mybir.dt.bfloat16
AF = mybir.ActivationFunctionType
ALU = mybir.AluOpType

NCORE = 8
DM = 4096
NTOK = 8192
TPC = 1024
HALO = 64
XT = TPC + HALO
NXT = 9
H = 32
KVH = 8
NKC = 32
CHUNK = 128
NCH = 8
WIN = 192
NEXP = 28
NER = 224
GINV = 1.0 / 16.0
ISQD = float(1.0 / np.sqrt(128.0))
EPS = 1e-5

TAPS = {}
ENG2SEM = {"tensor": "pe", "vector": "dve", "scalar": "act", "gpsimd": "pool"}
CSEMS = ("pe", "dve", "act", "pool")


def xm(kc, col=0):
    return kc * TPC + col


def build_program(taps=None, phases="ACBWF", a_limit=99, c_limit=99):
    import os
    ASTG = int(os.environ.get("ASTG", "9"))
    taps = dict(taps if taps is not None else TAPS)
    nc = bass.Bass("TRN2", target_bir_lowering=False)

    x_in = nc.dram_tensor("x", [XT, DM], F32, kind="ExternalInput").ap()
    wq = nc.dram_tensor("wq", [DM, DM], F32, kind="ExternalInput").ap()
    wk = nc.dram_tensor("wk", [DM, KVH * 128], F32, kind="ExternalInput").ap()
    wv = nc.dram_tensor("wv", [DM, KVH * 128], F32, kind="ExternalInput").ap()
    wo = nc.dram_tensor("wo", [DM, DM], F32, kind="ExternalInput").ap()
    bwt = nc.dram_tensor("bwt", [DM, DM], F32, kind="ExternalInput").ap()
    gwt = nc.dram_tensor("gwt", [DM, NEXP], F32, kind="ExternalInput").ap()
    lat = nc.dram_tensor("lat", [DM, NER], F32, kind="ExternalInput").ap()
    lbt = nc.dram_tensor("lbt", [NER, DM], F32, kind="ExternalInput").ap()
    identh = nc.dram_tensor("identh", [128, 128], BF16, kind="ExternalInput").ap()
    identfh = nc.dram_tensor("identfh", [128, 128], F32, kind="ExternalInput").ap()
    rench = nc.dram_tensor("rench", [128, 128], BF16, kind="ExternalInput").ap()
    cosh = nc.dram_tensor("cosh", [128, XT], BF16, kind="ExternalInput").ap()
    sinh = nc.dram_tensor("sinh", [128, XT], BF16, kind="ExternalInput").ap()
    maskh = nc.dram_tensor("maskh", [128, 2 * WIN], BF16, kind="ExternalInput").ap()
    trih = nc.dram_tensor("trih", [128, 256], BF16, kind="ExternalInput").ap()
    onesh = nc.dram_tensor("onesh", [128, 2], BF16, kind="ExternalInput").ap()
    onesrh = nc.dram_tensor("onesrh", [1, 128], BF16, kind="ExternalInput").ap()
    onesbfh = nc.dram_tensor("onesbfh", [128, 1], F32, kind="ExternalInput").ap()
    onesrfh = nc.dram_tensor("onesrfh", [1, 128], F32, kind="ExternalInput").ap()
    cmaskh = nc.dram_tensor("cmaskh", [128, 16], F32, kind="ExternalInput").ap()
    xexph = nc.dram_tensor("xexph", [NEXP, NER], BF16, kind="ExternalInput").ap()
    out = nc.dram_tensor("out", [TPC, DM], F32, kind="ExternalOutput").ap()

    omix_d = nc.dram_tensor("omix_d", [128, H * TPC], BF16)
    cc_in = nc.dram_tensor("cc_in", [KVH * 129, 128], F32)
    cc_out = nc.dram_tensor("cc_out", [NCORE * KVH * 129, 128], F32)

    tap_d = {}
    for name, (shape, dt) in taps.items():
        tap_d[name] = nc.dram_tensor("tap_" + name, list(shape), dt, kind="ExternalOutput").ap()
    tap_jobs = []

    def tap(name, ap):
        if name in tap_d:
            tap_jobs.append((name, ap))

    class Ctr:
        def __init__(self):
            self.sems = {}
            self.cnt = {}
            self.lastw = {}

        def add(self, es, *names):
            for n in names:
                self.sems[n] = es.enter_context(nc.semaphore(n))
                self.cnt[n] = 0

        def inc(self, name, instr, k=1):
            instr.then_inc(self.sems[name], k)
            self.cnt[name] += k
            return self.cnt[name]

        def wait(self, eng, name, v=None, key=None):
            tv = self.cnt[name] if v is None else v
            if tv <= 0:
                return
            if key is not None and not os.environ.get("NOTRACK"):
                if self.lastw.get((key, name), -1) >= tv:
                    return
                self.lastw[(key, name)] = tv
            eng.wait_ge(self.sems[name], tv)

    C = Ctr()
    glob = contextlib.ExitStack()

    def mk(es):
        return (lambda nm, shape, dt: es.enter_context(nc.sbuf_tensor(nm, shape, dt)),
                lambda nm, shape, dt: es.enter_context(nc.psum_tensor(nm, shape, dt)))

    cur_block = [None]

    class _Rec:

        def __init__(self, eng):
            self._e = eng
            self.last = None

        def __getattr__(self, name):
            f = getattr(self._e, name)
            if not callable(f):
                return f

            def wrap(*a, **kw):
                r = f(*a, **kw)
                if hasattr(r, "then_inc"):
                    self.last = r
                return r
            return wrap

    def step(eng, fn, dma=(), deps=None):
        sem = ENG2SEM[eng]

        def body(e):
            if deps is None:
                for w in CSEMS:
                    C.wait(e, w, key=eng)
            else:
                C.wait(e, sem, key=eng)
                for wsem, wval in deps:
                    C.wait(e, wsem, wval, key=eng)
            for wsem, wval in dma:
                e.wait_ge(C.sems[wsem], wval)
            rec = _Rec(e)
            fn(rec)
            if rec.last is None:
                rec.last = e.nop() if eng in ("tensor", "scalar") else e.engine_nop()
            C.inc(sem, rec.last)
        getattr(cur_block[0], eng)(body)
        return C.cnt[sem]

    with glob:
        sbg, _psg = mk(glob)
        C.add(glob, "pe", "dve", "act", "pool", "cc")

        class Stream:
            def __init__(self, base, n):
                self.names = [f"{base}{i}" for i in range(n)]
                C.add(glob, *self.names)
                self.n = n
                self.k = 0

            def issue(self, eng, mk_instr):
                nm = self.names[self.k % self.n]
                if self.k >= self.n:
                    eng.wait_ge(C.sems[nm], 16 * (self.k // self.n))
                C.inc(nm, mk_instr(), 16)
                tok = (nm, C.cnt[nm])
                self.k += 1
                return tok

            def wait(self, eng, tok, key=None):
                C.wait(eng, tok[0], tok[1], key=key)

            def wait_all(self, eng, key=None):
                for nm in self.names:
                    C.wait(eng, nm, key=key)

        st_w = Stream("wdma", 4)
        st_g = Stream("gdma", 2)
        st_a = Stream("adma", 2)
        st_o = Stream("odma", 2)
        st_c = Stream("cdma", 2)
        streams = (st_w, st_g, st_a, st_o, st_c)

        ident = sbg("ident", [128, 128], BF16)
        identf = sbg("identf", [128, 128], F32)
        renc = sbg("renc", [128, 128], BF16)
        costab = sbg("costab", [128, XT], BF16)
        sintab = sbg("sintab", [128, XT], BF16)
        maskswa = sbg("maskswa", [128, 2 * WIN], BF16)
        trimask = sbg("trimask", [128, 256], BF16)
        onesb = sbg("onesb", [128, 2], BF16)
        onesr = sbg("onesr", [1, 128], BF16)
        onesbf = sbg("onesbf", [128, 1], F32)
        onesrf = sbg("onesrf", [1, 128], F32)
        cmask = sbg("cmask", [128, 16], F32)
        xam = sbg("xam", [128, NKC * TPC], BF16)
        rowsc = sbg("rowsc", [1, 2048], F32)
        colsc = sbg("colsc", [128, 16], F32)

        def end_block(block):
            jobs = list(tap_jobs)
            tap_jobs.clear()

            @block.sync
            def _(s):
                for nm in CSEMS:
                    if C.cnt[nm]:
                        C.wait(s, nm)
                for st in streams:
                    st.wait_all(s)
                if C.cnt["cc"]:
                    C.wait(s, "cc")
                for name, ap in jobs:
                    st_o.issue(s, lambda s=s, name=name, ap=ap: s.dma_start(tap_d[name], ap))
                st_o.wait_all(s)

            @block.gpsimd
            def _(g):
                for nm in CSEMS + ("cc",):
                    if C.cnt[nm]:
                        C.wait(g, nm)
                for st in streams:
                    st.wait_all(g)

        es_halo = contextlib.ExitStack()
        sbh, _ = mk(es_halo)
        xah = sbh("xah", [128, NKC * HALO], BF16)

        def pin_act_table(block):
            return

        with contextlib.ExitStack() as es0:
            sb0, ps0 = mk(es0)
            xtok = sb0("xtok", [128, 2 * DM], BF16)
            ptb0 = [ps0(f"ptb0_{i}", [128, 1024], BF16) for i in range(2)]

            with nc.Block() as block:
                cur_block[0] = block
                pin_act_table(block)

                const_toks = []

                @block.sync
                def _(s):
                    for dst, src in ((ident, identh), (identf, identfh), (renc, rench),
                                     (costab, cosh), (sintab, sinh), (maskswa, maskh),
                                     (trimask, trih), (onesb, onesh), (onesr, onesrh),
                                     (onesbf, onesbfh), (onesrf, onesrfh),
                                     (cmask, cmaskh)):
                        const_toks.append(st_a.issue(s, lambda s=s, dst=dst, src=src:
                                                     s.dma_start(dst[:, :], src)))

                xtok_done = []
                tile_tr_done = []
                for t_i in range(NXT):
                    rows = 128 if t_i < 8 else 64

                    def dma_body(g, t_i=t_i, rows=rows):
                        if t_i >= 2:
                            g.wait_ge(C.sems["pe"], tile_tr_done[t_i - 2])
                        xtok_done.append(st_g.issue(g, lambda g=g, t_i=t_i, rows=rows: g.dma_start(
                            xtok[0:rows, bass.ts(t_i % 2, DM)],
                            x_in[t_i * 128:t_i * 128 + rows, :])))
                    block.gpsimd(dma_body)

                    for grp in range(4):
                        def tr_body(t, t_i=t_i, rows=rows, grp=grp):
                            if grp == 0:
                                st_a.wait_all(t)
                                st_g.wait(t, xtok_done[t_i])
                            for kc8 in range(8):
                                kc = grp * 8 + kc8
                                t.transpose(ptb0[t_i % 2][:, kc8 * 128:kc8 * 128 + rows],
                                            xtok[0:rows, (t_i % 2) * DM + kc * 128:(t_i % 2) * DM + (kc + 1) * 128],
                                            ident[0:rows, 0:rows])
                        tr_pe = step("tensor", tr_body)

                        def ev_body(v, t_i=t_i, rows=rows, grp=grp):
                            for kc8 in range(8):
                                kc = grp * 8 + kc8
                                srcp = ptb0[t_i % 2][:, kc8 * 128:kc8 * 128 + rows]
                                if t_i == 0:
                                    v.tensor_copy(xah[:, kc * HALO:(kc + 1) * HALO], srcp[:, 0:64])
                                    v.tensor_copy(xam[:, xm(kc, 0):xm(kc, 64)], srcp[:, 64:128])
                                else:
                                    v.tensor_copy(xam[:, xm(kc, t_i * 128 - 64):xm(kc, t_i * 128 - 64 + rows)],
                                                  srcp)
                        step("vector", ev_body)
                    tile_tr_done.append(tr_pe)

                tap("xam0", xam[:, 0:TPC])
                tap("xah0", xah[:, 0:HALO])
                end_block(block)

        ret = contextlib.ExitStack()
        sbr = lambda nm, shape, dt: ret.enter_context(
            nc.sbuf_tensor(nm, shape, dt, side="right"))
        sk = sbr("skt", [128, KVH * XT], BF16)
        vtk = sbr("vtk", [128, KVH * NXT * 128], BF16)
        ktl = sbr("ktl", [128, KVH * XT], BF16)
        vtk2 = sbr("vtk2", [128, KVH * NCH * 128], BF16)
        epp = sbr("epp", [128, KVH * TPC], BF16)
        ebc = sbr("ebc", [128, KVH * NCH], F32)
        ecm = sbr("ecm", [128, KVH * NCH], F32)
        fhat = sbr("fhat", [128, KVH * NCH], F32)
        tseg = sbr("tseg", [128, KVH * 128], F32)
        bsum = sbr("bsum", [128, KVH], F32)
        ebsg = sbr("ebsg", [128, KVH], F32)
        ebsgr = sbr("ebsgr", [KVH, 128], F32)

        def vt(h, t):
            return (h * NXT + t) * 128

        if "A" in phases:
            with contextlib.ExitStack() as esA:
                sbA, psA = mk(esA)
                wbufA = sbA("wbufA", [128, 2 * 16 * 128], BF16)
                kf = sbA("kf", [128, TPC], F32)
                kb = sbA("kb", [128, XT], BF16)
                eqf = sbA("eqf", [128, XT], F32)
                gsab = sbA("gsab", [128, 512], F32)
                gsmn = sbA("gsmn", [128, 512], F32)
                cumv = sbA("cumv", [128, CHUNK], F32)
                ksc = sbA("ksc", [128, CHUNK], F32)
                ene = sbA("ene", [128, CHUNK], F32)
                vb = sbA("vb", [128, XT], BF16)
                ktok = sbA("ktokA", [128, 128], BF16)
                sct = sbA("sctA", [128, 128], F32)
                ru = sbA("ruA", [128, 512], F32)
                rt = sbA("rtA", [128, 512], F32)
                pga = [psA(f"pga{i}", [128, 512], F32) for i in range(2)]
                pn64 = psA("pn64", [128, 64], F32)
                pmisc = psA("pmiscA", [128, 512], F32)
                pm2 = psA("pm2A", [128, 512], F32)
                ptbA = psA("ptbA", [128, 1024], BF16)
                ptbA2 = psA("ptbA2", [128, 1024], BF16)

                with nc.Block() as block:
                    cur_block[0] = block
                    wdone = []
                    wrel = [None] * 32

                    def wdma(g, s):
                        src = wk if s < 16 else wv
                        hh = (s % 16) // 2
                        hf = s % 2
                        if s >= 2:
                            g.wait_ge(C.sems["pe"], wrel[s - 2])
                        wdone.append(st_w.issue(g, lambda g=g, s=s, src=src, hf=hf, hh=hh: g.dma_start(
                            wbufA[:, bass.ts(s % 2, 16 * 128)].rearrange(
                                "p (kc n) -> p kc n", kc=16),
                            src.rearrange("(kc p) n -> p kc n", p=128)[:, 16 * hf:16 * hf + 16,
                                                                      hh * 128:(hh + 1) * 128])))

                    block.gpsimd(lambda g: (wdma(g, 0), wdma(g, 1)))

                    def z0(v):
                        for h in range(KVH):
                            v.memset(ktl[:, h * XT:h * XT + HALO], 0)
                            v.memset(vtk[64:128, vt(h, 8):vt(h, 8) + 128], 0)
                        v.memset(tseg[:, :], 0)
                        v.memset(bsum[:, :], 0)
                    step("vector", z0)

                    tA = dict(eqkb=[0] * 16, kfev=[0] * 16, vev=[0] * 16,
                              abslast=[0] * 16, ln=[0] * 16, rtlast=[0] * 16,
                              gssub=0, chunk1=0, chunk2=0, ktl=[0] * 8, ktllast=[0] * 8,
                              ptbacopy=[0] * 16, vtk2last=[0] * 8)
                    for i in range(min(16, a_limit)):
                        isK = i < 8
                        h = i % 8

                        def gemm(t, i=i):
                            for hf in range(2):
                                st_w.wait(t, wdone[2 * i + hf])
                                for k16 in range(16):
                                    kcc = hf * 16 + k16
                                    wsl = wbufA[:, (hf % 2) * 2048 + k16 * 128:(hf % 2) * 2048 + (k16 + 1) * 128]
                                    st = (kcc == 0)
                                    sp = (kcc == NKC - 1)
                                    t.matmul(pn64[:, 0:64], wsl,
                                             xah[:, kcc * HALO:(kcc + 1) * HALO],
                                             start=st, stop=sp)
                                    t.matmul(pga[0][:, :], wsl, xam[:, xm(kcc, 0):xm(kcc, 512)],
                                             start=st, stop=sp)
                                    t.matmul(pga[1][:, :], wsl, xam[:, xm(kcc, 512):xm(kcc, 1024)],
                                             start=st, stop=sp)
                                if hf == 0:
                                    wrel[2 * i] = C.inc("pe", t.last)
                            wrel[2 * i + 1] = C.cnt["pe"] + 1
                        gdep = []
                        if i > 0:
                            gdep = [("act", tA["eqkb"][i - 1] if i - 1 < 8 else tA["vev"][i - 1]),
                                    ("dve", tA["kfev"][i - 1])]
                        t_gemm = step("tensor", gemm, deps=gdep)
                        if 2 * i + 2 < 32:
                            block.gpsimd(lambda g, i=i: (wdma(g, 2 * i + 2), wdma(g, 2 * i + 3)))

                        if isK:
                            def eqkb(sc):
                                sc.activation(eqf[:, 0:64], pn64[:, 0:64], AF.Exp)
                                sc.activation(eqf[:, 64:576], pga[0][:, :], AF.Exp)
                                sc.activation(eqf[:, 576:1088], pga[1][:, :], AF.Exp)
                                sc.copy(kb[:, 0:64], pn64[:, 0:64])
                                sc.copy(kb[:, 64:576], pga[0][:, :])
                                sc.copy(kb[:, 576:1088], pga[1][:, :])
                            tA["eqkb"][i] = step("scalar", eqkb, deps=[
                                ("pe", t_gemm), ("dve", tA["rtlast"][i - 1] if i else 0)])

                            def kfev(v):
                                v.tensor_copy(kf[:, 0:512], pga[0][:, :])
                                v.tensor_copy(kf[:, 512:1024], pga[1][:, :])
                            tA["kfev"][i] = step("vector", kfev, deps=[
                                ("pe", t_gemm), ("act", tA["abslast"][i - 1] if i else 0)])

                            if ASTG < 2:
                                continue
                            t_sums = step("tensor", lambda t: (
                                t.matmul(pmisc[0:1, 0:512], onesbf[:, 0:1], eqf[:, 64:576],
                                         start=True, stop=True),
                                t.matmul(pmisc[32:33, 0:512], onesbf[:, 0:1], eqf[:, 576:1088],
                                         start=True, stop=True)),
                                deps=[("act", tA["eqkb"][i]),
                                      ("dve", tA["rtlast"][i - 1] if i else 0)])
                            tA["ln"][i] = step("scalar", lambda sc: (
                                sc.activation(rowsc[0:1, 0:512], pmisc[0:1, 0:512], AF.Ln),
                                sc.activation(rowsc[0:1, 512:1024], pmisc[32:33, 0:512], AF.Ln)),
                                deps=[("pe", t_sums)])
                            t_expneg = step("scalar", lambda sc: (
                                sc.activation(rowsc[0:1, 1024:1536], rowsc[0:1, 0:512], AF.Exp, scale=-1.0),
                                sc.activation(rowsc[0:1, 1536:2048], rowsc[0:1, 512:1024], AF.Exp, scale=-1.0)),
                                deps=[])

                            if ASTG < 3:
                                continue
                            for half in range(2):
                                t_abs = step("scalar", lambda sc, half=half: sc.activation(
                                    gsab[:, :], kf[:, half * 512:(half + 1) * 512], AF.Abs),
                                    deps=[("dve", tA["kfev"][i]),
                                          ("dve", tA["gssub"])])
                                step("scalar", lambda sc: sc.activation(
                                    gsab[:, :], gsab[:, :], AF.Exp, scale=-1.0), deps=[])
                                t_ln1p = step("scalar", lambda sc: sc.activation(
                                    gsab[:, :], gsab[:, :], AF.Ln, bias=1.0), deps=[])
                                tA["abslast"][i] = t_ln1p
                                step("vector", lambda v, half=half: v.tensor_scalar(
                                    gsmn[:, :], kf[:, half * 512:(half + 1) * 512], 0.0, None, ALU.min),
                                    deps=[])
                                tA["gssub"] = step("vector", lambda v: v.tensor_tensor(
                                    gsmn[:, :], gsmn[:, :], gsab[:, :], ALU.subtract),
                                    deps=[("act", t_ln1p)])
                                t_rbc = step("tensor", lambda t, half=half: t.matmul(
                                    pm2[:, 0:512], onesrf[0:1, :],
                                    rowsc[0:1, 1024 + half * 512:1536 + half * 512],
                                    start=True, stop=True),
                                    deps=[("act", t_expneg), ("dve", tA["chunk1"])])

                                for cc4 in range(4):
                                    c = half * 4 + cc4

                                    def chunk1(v, c=c, cc4=cc4):
                                        v.tensor_tensor(ksc[:, :],
                                                        eqf[:, 64 + c * 128:64 + (c + 1) * 128],
                                                        pm2[:, cc4 * 128:(cc4 + 1) * 128], ALU.mult)
                                        v.tensor_tensor_scan(cumv[:, :],
                                                             gsmn[:, cc4 * 128:(cc4 + 1) * 128],
                                                             gsmn[:, cc4 * 128:(cc4 + 1) * 128], 0.0,
                                                             ALU.add, ALU.bypass)
                                    tA["chunk1"] = step("vector", chunk1,
                                                        deps=[("pe", t_rbc), ("act", tA["eqkb"][i]),
                                                              ("act", tA["chunk2"])])

                                    def chunk1b(v, h=h):
                                        v.tensor_scalar(colsc[:, 0:1], cumv[:, 64:65], -GINV, None, ALU.mult)
                                        v.tensor_scalar(colsc[:, 1:2], cumv[:, 64:65], GINV, None, ALU.mult)
                                        v.tensor_scalar(colsc[:, 2:3], cumv[:, 127:128], GINV, None, ALU.mult)
                                        v.tensor_tensor(colsc[:, 3:4], cumv[:, 127:128],
                                                        cumv[:, 64:65], ALU.subtract)
                                        v.tensor_tensor(bsum[:, h:h + 1], bsum[:, h:h + 1],
                                                        cumv[:, 127:128], ALU.add)
                                    t_c1b = step("vector", chunk1b, deps=[("act", tA["chunk2"])])

                                    def chunk2(sc, h=h, c=c):
                                        sc.activation(epp[:, h * TPC + c * 128:h * TPC + (c + 1) * 128],
                                                      cumv[:, :], AF.Exp, scale=GINV, bias=colsc[:, 0:1])
                                        sc.activation(ene[:, :], cumv[:, :], AF.Exp, scale=-GINV,
                                                      bias=colsc[:, 1:2])
                                        sc.activation(ebc[:, h * NCH + c:h * NCH + c + 1],
                                                      cumv[:, 127:128], AF.Exp, scale=GINV)
                                        sc.activation(ecm[:, h * NCH + c:h * NCH + c + 1],
                                                      cumv[:, 64:65], AF.Exp, scale=GINV)
                                        sc.activation(fhat[:, h * NCH + c:h * NCH + c + 1],
                                                      colsc[:, 3:4], AF.Exp, scale=GINV)
                                    tA["chunk2"] = step("scalar", chunk2,
                                                        deps=[("dve", t_c1b),
                                                              ("dve", tA["ktl"][h])])

                                    tA["ktl"][h] = step("vector", lambda v, h=h, c=c: v.tensor_tensor(
                                        ktl[:, h * XT + 64 + c * 128:h * XT + 64 + (c + 1) * 128],
                                        ksc[:, :], ene[:, :], ALU.mult),
                                        deps=[("act", tA["chunk2"])])
                            tA["ktllast"][h] = tA["ktl"][h]

                            if ASTG < 4:
                                continue
                            t_rt = tA["rtlast"][i - 1] if i else 0
                            for (c0, cn) in ((0, 64), (64, 512), (576, 512)):
                                t_rk = step("tensor", lambda t, c0=c0, cn=cn: t.matmul(
                                    pmisc[:, 0:cn], renc[:, :], kb[:, c0:c0 + cn],
                                    start=True, stop=True),
                                    deps=[("act", tA["ln"][i]), ("dve", t_rt)])
                                t_rt = step("vector", lambda v, c0=c0, cn=cn: (
                                    v.tensor_tensor(rt[:, 0:cn], pmisc[:, 0:cn],
                                                    sintab[:, c0:c0 + cn], ALU.mult),
                                    v.tensor_tensor(ru[:, 0:cn], kb[:, c0:c0 + cn],
                                                    costab[:, c0:c0 + cn], ALU.mult)),
                                    deps=[("pe", t_rk)])
                                step("vector", lambda v, h=h, c0=c0, cn=cn: v.tensor_tensor(
                                    sk[:, h * XT + c0:h * XT + c0 + cn],
                                    ru[:, 0:cn], rt[:, 0:cn], ALU.add), deps=[])
                            tA["rtlast"][i] = t_rt
                        else:
                            def vev(sc):
                                sc.copy(vb[:, 0:64], pn64[:, 0:64])
                                sc.copy(vb[:, 64:576], pga[0][:, :])
                                sc.copy(vb[:, 576:1088], pga[1][:, :])
                            tA["vev"][i] = step("scalar", vev, deps=[("pe", t_gemm)])
                            tA["kfev"][i] = C.cnt["dve"]

                            cpb = [tA["ptbacopy"][i - 1]] * 2
                            for j_t in range(NXT + NCH):
                                bank = [ptbA, ptbA2][j_t % 2]
                                if j_t < NXT:
                                    tt = j_t
                                    cols = 128 if tt < 8 else 64
                                    src_ap = None
                                    t_tr = step("tensor", lambda t, tt=tt, cols=cols, bank=bank: t.transpose(
                                        bank[0:cols, 0:128],
                                        vb[:, tt * 128:tt * 128 + cols], ident[:, :]),
                                        deps=[("act", tA["vev"][i]), ("dve", cpb[j_t % 2])])
                                    cpb[j_t % 2] = step(
                                        "vector", lambda v, h=h, tt=tt, cols=cols, bank=bank: v.tensor_copy(
                                            vtk[0:cols, vt(h, tt):vt(h, tt) + 128],
                                            bank[0:cols, 0:128]),
                                        deps=[("pe", t_tr)])
                                else:
                                    cc = j_t - NXT
                                    t_tr = step("tensor", lambda t, cc=cc, bank=bank: t.transpose(
                                        bank[:, 0:128],
                                        vb[:, 64 + cc * 128:64 + (cc + 1) * 128], ident[:, :]),
                                        deps=[("dve", cpb[j_t % 2])])
                                    cpb[j_t % 2] = step(
                                        "vector", lambda v, h=h, cc=cc, bank=bank: v.tensor_copy(
                                            vtk2[:, (h * NCH + cc) * 128:(h * NCH + cc + 1) * 128],
                                            bank[:, 0:128]),
                                        deps=[("pe", t_tr)])
                            tA["ptbacopy"][i] = max(cpb)
                            tA["vtk2last"][h] = max(cpb)

                    for h in range(KVH if a_limit >= 17 else 0):
                        for c in range(NCH):
                            step("tensor", lambda t, h=h, c=c: t.transpose(
                                ptbA[:, 0:128],
                                ktl[:, h * XT + 64 + c * 128:h * XT + 64 + (c + 1) * 128],
                                ident[:, :]))
                            step("vector", lambda v: v.tensor_copy(ktok[:, :], ptbA[:, 0:128]))
                            step("tensor", lambda t, h=h, c=c: t.matmul(
                                pmisc[:, 0:128], ktok[:, :],
                                vtk2[:, (h * NCH + c) * 128:(h * NCH + c + 1) * 128],
                                start=True, stop=True))
                            step("vector", lambda v, h=h, c=c: v.tensor_tensor(
                                sct[:, :], pmisc[:, 0:128],
                                fhat[:, h * NCH + c:h * NCH + c + 1].broadcast_to((128, 128)),
                                ALU.mult))
                            step("vector", lambda v, h=h, c=c: v.tensor_tensor(
                                tseg[:, h * 128:(h + 1) * 128],
                                tseg[:, h * 128:(h + 1) * 128],
                                ebc[:, h * NCH + c:h * NCH + c + 1].broadcast_to((128, 128)),
                                ALU.mult))
                            step("vector", lambda v, h=h: v.tensor_tensor(
                                tseg[:, h * 128:(h + 1) * 128],
                                tseg[:, h * 128:(h + 1) * 128],
                                sct[:, :], ALU.add))

                    if a_limit >= 18:
                        step("scalar", lambda sc: sc.activation(ebsg[:, :], bsum[:, :], AF.Exp, scale=GINV))
                        step("tensor", lambda t: t.transpose(
                            pmisc[0:KVH, 0:128], ebsg[:, :], identf[:, :]))
                        step("vector", lambda v: v.tensor_copy(ebsgr[:, :], pmisc[0:KVH, 0:128]))

                    def ccdma(g):
                        if a_limit < 18:
                            return
                        for nm in CSEMS:
                            C.wait(g, nm)
                        for h in range(KVH):
                            st_c.issue(g, lambda g=g, h=h: g.dma_start(
                                cc_in[h * 129:h * 129 + 128, :], tseg[:, h * 128:(h + 1) * 128]))
                            st_c.issue(g, lambda g=g, h=h: g.dma_start(
                                cc_in[h * 129 + 128:h * 129 + 129, :], ebsgr[h:h + 1, :]))
                    block.gpsimd(ccdma)

                    tap("sk0", sk[:, 0:XT])
                    tap("vtk0", vtk[:, 0:NXT * 128])
                    tap("ktl0", ktl[:, 0:XT])
                    tap("vtk2_0", vtk2[:, 0:NCH * 128])
                    tap("fhat", fhat[:, :])
                    tap("epp0", epp[:, 0:TPC])
                    tap("tseg", tseg[:, :])
                    tap("ebsg", ebsg[:, :])
                    tap("ebc", ebc[:, :])
                    tap("ecm", ecm[:, :])
                    end_block(block)

        es_halo.close()

        if "C" in phases:
            with contextlib.ExitStack() as esC:
                sbC, psC = mk(esC)
                ccsb = sbC("ccsb", [128, 4 * 129], F32)
                brsb = sbC("brsb", [1, 4 * 128], F32)
                pmC = psC("pmC", [128, 4], F32)

                with nc.Block() as block:
                    cur_block[0] = block

                    def coll(g):
                        if c_limit < 0:
                            return
                        C.inc("cc", g.collective_compute(
                            "AllGather", ALU.bypass,
                            replica_groups=[list(range(NCORE))],
                            ins=[cc_in.ap().opt()],
                            outs=[cc_out.ap().opt()]))
                    block.gpsimd(coll)

                    step("vector", lambda v: v.memset(tseg[:, :], 0))
                    ldj_toks = []
                    comp_dve = []
                    btr_pe = []

                    NB = 4
                    for h in range(KVH):
                        for j in range(NCORE):
                            it = h * NCORE + j
                            b = it % NB

                            def ldj(g, h=h, j=j, it=it, b=b):
                                if it == 0:
                                    C.wait(g, "cc")
                                    C.wait(g, "dve")
                                if it >= NB:
                                    C.wait(g, "dve", comp_dve[it - NB])
                                base = (j * KVH + h) * 129
                                ldj_toks.append(st_g.issue(g, lambda g=g, base=base, b=b: g.dma_start(
                                    ccsb[:, b * 129:b * 129 + 128], cc_out[base:base + 128, :])))
                                ldj_toks.append(st_g.issue(g, lambda g=g, base=base, b=b: g.dma_start(
                                    brsb[0:1, b * 128:(b + 1) * 128], cc_out[base + 128:base + 129, :])))
                            block.gpsimd(ldj)
                            dv = list(ldj_toks[-2:])

                            def btr(t, b=b, dv=dv, it=it):
                                st_g.wait(t, dv[1])
                                if it >= NB:
                                    t.wait_ge(C.sems["dve"], comp_dve[it - NB])
                                t.transpose(pmC[0:128, b:b + 1], brsb[0:1, b * 128:(b + 1) * 128], identf[0:1, 0:1])
                            btr_pe.append(step("tensor", btr, deps=[]))

                            def comp1(v, h=h, j=j, b=b, dv=dv):
                                st_g.wait(v, dv[0])
                                v.tensor_tensor(colsc[:, 0:1], pmC[0:128, b:b + 1],
                                                cmask[:, j:j + 1], ALU.mult)
                            step("vector", comp1, deps=[("pe", btr_pe[-1])])

                            def comp2(v, h=h, j=j, b=b):
                                v.tensor_tensor(colsc[:, 0:1], colsc[:, 0:1],
                                                cmask[:, 8 + j:9 + j], ALU.add)
                                v.tensor_tensor(ccsb[:, b * 129:b * 129 + 128],
                                                ccsb[:, b * 129:b * 129 + 128],
                                                cmask[:, j:j + 1].broadcast_to((128, 128)), ALU.mult)
                            step("vector", comp2, deps=[])

                            step("vector", lambda v, h=h: v.tensor_tensor(
                                tseg[:, h * 128:(h + 1) * 128],
                                tseg[:, h * 128:(h + 1) * 128],
                                colsc[:, 0:1].broadcast_to((128, 128)), ALU.mult), deps=[])
                            comp_dve.append(step("vector", lambda v, h=h, b=b: v.tensor_tensor(
                                tseg[:, h * 128:(h + 1) * 128],
                                tseg[:, h * 128:(h + 1) * 128],
                                ccsb[:, b * 129:b * 129 + 128], ALU.add), deps=[]))

                    tap("sin", tseg[:, :])
                    end_block(block)

        if "B" in phases:
            with contextlib.ExitStack() as esB:
                sbB, psB = mk(esB)
                wbufB = sbB("wbufB", [128, 2 * 8 * 128], BF16)
                qb = sbB("qb", [128, TPC], BF16)
                eqq = sbB("eqq", [128, TPC], BF16)
                qsb = sbB("qsb", [128, 4 * TPC], BF16)
                sqb = sbB("sqb", [128, 4 * TPC], BF16)
                ru = sbB("ruB", [128, 512], F32)
                rt = sbB("rtB", [128, 512], BF16)
                stl = sbB("stl", [128, NCH * 128], BF16)
                ktokg = sbB("ktokg", [128, NCH * 128], BF16)
                scrB = sbB("scrB", [128, NCH * 128], BF16)
                sctB8 = scrB
                epg = sbB("epg", [128, 2 * 4 * WIN], BF16)
                p1t = sbB("p1t", [128, 512], BF16)
                p2t = sbB("p2t", [128, 512], BF16)
                ab1 = scrB[:, 0:512]
                ab2 = scrB[:, 512:1024]
                sums = sbB("sums", [128, 2 * 8], F32)
                omst = sbB("omst", [128, 512], BF16)
                pq = [psB(f"pq{i}", [128, 512], F32) for i in range(2)]
                pmB = psB("pmB", [128, 512], F32)
                pm3 = psB("pm3B", [128, 512], F32)
                poB = psB("poB", [128, 512], F32)
                pAT = psB("pAT", [128, 512], F32)
                ptbB = psB("ptbB", [128, 1024], BF16)

                with nc.Block() as block:
                    cur_block[0] = block
                    wqdone = []
                    wqrel = [None] * (4 * H)
                    om_dma = []

                    def wqdma(g, s):
                        qh = s // 4
                        qq = s % 4
                        if s >= 2:
                            g.wait_ge(C.sems["pe"], wqrel[s - 2])
                        wqdone.append(st_w.issue(g, lambda g=g, s=s, qq=qq, qh=qh: g.dma_start(
                            wbufB[:, bass.ts(s % 2, 8 * 128)].rearrange(
                                "p (kc n) -> p kc n", kc=8),
                            wq.rearrange("(kc p) n -> p kc n", p=128)[:, 8 * qq:8 * qq + 8,
                                                                     qh * 128:(qh + 1) * 128])))

                    block.gpsimd(lambda g: (wqdma(g, 0), wqdma(g, 1)))

                    tB = dict(eqq=0, scoreexp=[0, 0], sctlast=0, pm3read=0, qsbreadpe=0,
                              sqreadpe=0, rt=0, oev=0, ptev=0, abev=0, pnorm=[0, 0],
                              o=0, sqaddlast=0, qt2last=0, lnlast=0, ptrlast=0,
                              pmask=[0, 0], stlread=0, tmmlast=0, ktokev=0)
                    post = [None]
                    for g_i in range(KVH):
                        def ktrs(t, g_i=g_i):
                            for c in range(NCH):
                                t.transpose(ptbB[:, bass.ts(c % 8, 128)],
                                            ktl[:, g_i * XT + 64 + c * 128:g_i * XT + 64 + (c + 1) * 128],
                                            ident[:, :])
                        t_ktr = step("tensor", ktrs,
                                     deps=[("act", max(tB["ptev"], tB["ktokev"]))])
                        t_kte = step("scalar", lambda sc: sc.copy(ktokg[:, :], ptbB[:, :]),
                                     deps=[("pe", t_ktr)])
                        tB["ktokev"] = t_kte

                        def tmms(t, g_i=g_i):
                            for c in range(NCH):
                                pb = [poB, pAT][c // 4]
                                t.matmul(pb[:, (c % 4) * 128:(c % 4 + 1) * 128],
                                         ktokg[:, bass.ts(c, 128)],
                                         vtk2[:, (g_i * NCH + c) * 128:(g_i * NCH + c + 1) * 128],
                                         start=True, stop=True)
                        t_tmm = step("tensor", tmms,
                                     deps=[("act", t_kte), ("act", tB["oev"]),
                                           ("dve", tB["abev"])])
                        tB["tmmlast"] = t_tmm
                        t_sct = step("vector", lambda v, g_i=g_i: (
                            v.tensor_tensor(sctB8[:, 0:512].rearrange("p (c n) -> p c n", c=4),
                                            poB[:, :].rearrange("p (c n) -> p c n", c=4),
                                            fhat[:, g_i * NCH:g_i * NCH + 4].unsqueeze(2).broadcast_to((128, 4, 128)),
                                            ALU.mult),
                            v.tensor_tensor(sctB8[:, 512:1024].rearrange("p (c n) -> p c n", c=4),
                                            pAT[:, :].rearrange("p (c n) -> p c n", c=4),
                                            fhat[:, g_i * NCH + 4:g_i * NCH + 8].unsqueeze(2).broadcast_to((128, 4, 128)),
                                            ALU.mult)),
                            deps=[("pe", t_tmm)])
                        for c in range(NCH):
                            step("vector", lambda v, g_i=g_i, c=c: v.scalar_tensor_tensor(
                                stl[:, c * 128:(c + 1) * 128],
                                tseg[:, g_i * 128:(g_i + 1) * 128], 0.5,
                                ecm[:, g_i * NCH + c:g_i * NCH + c + 1].broadcast_to((128, 128)),
                                ALU.mult, ALU.mult),
                                deps=[("pe", tB["stlread"])])
                            step("vector", lambda v, g_i=g_i, c=c: v.tensor_tensor(
                                tseg[:, g_i * 128:(g_i + 1) * 128],
                                tseg[:, g_i * 128:(g_i + 1) * 128],
                                ebc[:, g_i * NCH + c:g_i * NCH + c + 1].broadcast_to((128, 128)),
                                ALU.mult), deps=[])
                            tB["sctlast"] = step("vector", lambda v, g_i=g_i, c=c: v.tensor_tensor(
                                tseg[:, g_i * 128:(g_i + 1) * 128],
                                tseg[:, g_i * 128:(g_i + 1) * 128],
                                sctB8[:, c * 128:(c + 1) * 128], ALU.add), deps=[])

                        def make_post(qh, hh):
                            def post_fn():
                                t_sums = step("tensor", lambda t: (
                                    t.matmul(pmB[0:1, 0:512], onesb[:, 0:1], eqq[:, 0:512],
                                             start=True, stop=True),
                                    t.matmul(pmB[32:33, 0:512], onesb[:, 0:1], eqq[:, 512:1024],
                                             start=True, stop=True)),
                                    deps=[("act", tB["eqq"]), ("dve", tB["sctlast"])])
                                t_ln = step("scalar", lambda sc: (
                                    sc.activation(rowsc[0:1, 0:512], pmB[0:1, 0:512], AF.Ln),
                                    sc.activation(rowsc[0:1, 512:1024], pmB[32:33, 0:512], AF.Ln)),
                                    deps=[("pe", t_sums)])
                                tB["lnlast"] = t_ln
                                t_expneg = step("scalar", lambda sc: (
                                    sc.activation(rowsc[0:1, 1024:1536], rowsc[0:1, 0:512], AF.Exp, scale=-1.0),
                                    sc.activation(rowsc[0:1, 1536:2048], rowsc[0:1, 512:1024], AF.Exp, scale=-1.0)),
                                    deps=[])
                                for half in range(2):
                                    t_rbc = step("tensor", lambda t, half=half: t.matmul(
                                        pm3[:, 0:512], onesrf[0:1, :],
                                        rowsc[0:1, 1024 + half * 512:1536 + half * 512],
                                        start=True, stop=True),
                                        deps=[("act", t_expneg), ("dve", tB["pm3read"])])
                                    t_qt1 = step("vector", lambda v, half=half: v.tensor_tensor(
                                        ru[:, :],
                                        eqq[:, half * 512:(half + 1) * 512],
                                        pm3[:, 0:512], ALU.mult),
                                        deps=[("pe", t_rbc)])
                                    tB["pm3read"] = t_qt1
                                    tB["qt2last"] = step("vector", lambda v, g_i=g_i, hh=hh, half=half: v.tensor_tensor(
                                        qsb[:, hh * TPC + half * 512:hh * TPC + (half + 1) * 512],
                                        ru[:, :],
                                        epp[:, g_i * TPC + half * 512:g_i * TPC + (half + 1) * 512],
                                        ALU.mult),
                                        deps=[("pe", tB["qsbreadpe"])])
                                    t_rq = step("tensor", lambda t, half=half: t.matmul(
                                        pm3[:, 0:512], renc[:, :], qb[:, half * 512:(half + 1) * 512],
                                        start=True, stop=True),
                                        deps=[("act", tB["eqq"]), ("dve", t_qt1)])
                                    tB["rt"] = step("vector", lambda v, half=half: (
                                        v.tensor_tensor(rt[:, :], pm3[:, 0:512],
                                                        sintab[:, 64 + half * 512:64 + (half + 1) * 512], ALU.mult),
                                        v.tensor_tensor(ru[:, :], qb[:, half * 512:(half + 1) * 512],
                                                        costab[:, 64 + half * 512:64 + (half + 1) * 512], ALU.mult)),
                                        deps=[("pe", t_rq)])
                                    tB["pm3read"] = tB["rt"]
                                    tB["sqaddlast"] = step("vector", lambda v, hh=hh, half=half: v.tensor_tensor(
                                        sqb[:, hh * TPC + half * 512:hh * TPC + (half + 1) * 512],
                                        ru[:, :], rt[:, :], ALU.add),
                                        deps=[("pe", tB["sqreadpe"])])
                            return post_fn

                        for hh in range(4):
                            qh = 4 * g_i + hh

                            for gh in range(2):
                                def qgemm(t, qh=qh, gh=gh):
                                    for qq in (2 * gh, 2 * gh + 1):
                                        st_w.wait(t, wqdone[4 * qh + qq])
                                        for k8 in range(8):
                                            kcc = qq * 8 + k8
                                            wsl = wbufB[:, (qq % 2) * 1024 + k8 * 128:(qq % 2) * 1024 + (k8 + 1) * 128]
                                            for n in range(2):
                                                t.matmul(pq[n][:, :], wsl,
                                                         xam[:, xm(kcc, n * 512):xm(kcc, (n + 1) * 512)],
                                                         start=(kcc == 0), stop=(kcc == NKC - 1))
                                        if qq == 2 * gh:
                                            wqrel[4 * qh + qq] = C.inc("pe", t.last)
                                        else:
                                            wqrel[4 * qh + qq] = C.cnt["pe"] + 1
                                t_qg = step("tensor", qgemm,
                                            deps=[("act", max(tB["eqq"], max(tB["scoreexp"])))])
                                for s_n in range(4 * qh + 2 * gh + 2, min(4 * qh + 2 * gh + 4, 4 * H)):
                                    block.gpsimd(lambda g, s_n=s_n: wqdma(g, s_n))

                            if post[0] is not None:
                                post[0]()

                            tB["eqq"] = step("scalar", lambda sc: (
                                sc.activation(eqq[:, 0:512], pq[0][:, :], AF.Exp),
                                sc.activation(eqq[:, 512:1024], pq[1][:, :], AF.Exp),
                                sc.copy(qb[:, 0:512], pq[0][:, :]),
                                sc.copy(qb[:, 512:1024], pq[1][:, :])),
                                deps=[("pe", t_qg), ("dve", tB["rt"])])

                            post[0] = make_post(qh, hh)
                        post[0]()
                        post[0] = None

                        def stageA(c, g_i=g_i):
                            p = c % 2
                            eo = p * 4 * WIN

                            def scmm(t, g_i=g_i, c=c):
                                for hh in range(4):
                                    t.matmul(pq[hh // 2][:, (hh % 2) * WIN:(hh % 2 + 1) * WIN],
                                             sqb[:, hh * TPC + c * 128:hh * TPC + (c + 1) * 128],
                                             sk[:, g_i * XT + c * 128:g_i * XT + c * 128 + WIN],
                                             start=True, stop=True)
                            t_sc = step("tensor", scmm,
                                        deps=[("dve", tB["sqaddlast"]),
                                              ("act", tB["scoreexp"][(c + 1) % 2])])
                            tB["sqreadpe"] = t_sc
                            tB["scoreexp"][p] = step("scalar", lambda sc, eo=eo: (
                                sc.activation(epg[:, eo:eo + 2 * WIN], pq[0][:, 0:2 * WIN], AF.Exp, scale=ISQD),
                                sc.activation(epg[:, eo + 2 * WIN:eo + 4 * WIN], pq[1][:, 0:2 * WIN], AF.Exp, scale=ISQD)),
                                deps=[("pe", t_sc), ("dve", tB["pnorm"][p]),
                                      ("pe", tB["ptrlast"])])

                            moff = WIN if c == 0 else 0

                            def pmask(v, moff=moff, eo=eo, p=p):
                                for hh in range(4):
                                    v.scalar_tensor_tensor(
                                        epg[:, eo + hh * WIN:eo + (hh + 1) * WIN],
                                        epg[:, eo + hh * WIN:eo + (hh + 1) * WIN], 1.0,
                                        maskswa[:, moff:moff + WIN],
                                        ALU.mult, ALU.mult,
                                        accum_out=sums[:, p * 8 + hh:p * 8 + hh + 1])
                            tB["pmask"][p] = step("vector", pmask, deps=[("act", tB["scoreexp"][p])])
                            t_recip = step("vector", lambda v, p=p: v.reciprocal(
                                sums[:, p * 8 + 4:p * 8 + 8], sums[:, p * 8:p * 8 + 4]), deps=[])
                            t_bounce = step("scalar", lambda sc: None, deps=[("dve", t_recip)])

                            def pnorm(v, eo=eo, p=p):
                                for hh in range(4):
                                    v.tensor_scalar(epg[:, eo + hh * WIN:eo + (hh + 1) * WIN],
                                                    epg[:, eo + hh * WIN:eo + (hh + 1) * WIN],
                                                    sums[:, p * 8 + 4 + hh:p * 8 + 5 + hh], 0.5,
                                                    ALU.mult, ALU.mult)
                            tB["pnorm"][p] = step("vector", pnorm, deps=[("act", t_bounce)])

                        def stageB(c, g_i=g_i):
                            p = c % 2
                            eo = p * 4 * WIN

                            def ptr(t, eo=eo):
                                for hh in range(4):
                                    t.transpose(ptbB[:, hh * 128:(hh + 1) * 128],
                                                epg[:, eo + hh * WIN:eo + hh * WIN + 128], ident[:, :])
                                    t.transpose(ptbB[0:64, 512 + hh * 128:512 + (hh + 1) * 128],
                                                epg[:, eo + hh * WIN + 128:eo + hh * WIN + WIN], ident[:, :])
                            t_ptr = step("tensor", ptr,
                                         deps=[("dve", tB["pnorm"][p]),
                                               ("act", max(tB["ptev"], tB["ktokev"]))])
                            tB["ptrlast"] = t_ptr
                            tB["ptev"] = step("scalar", lambda sc: (
                                sc.copy(p1t[:, :], ptbB[:, 0:512]),
                                sc.copy(p2t[0:64, :], ptbB[0:64, 512:1024])),
                                deps=[("pe", t_ptr), ("pe", tB["o"])])

                            t_at = step("tensor", lambda t, g_i=g_i, c=c: (
                                t.matmul(pAT[:, :],
                                         ktl[:, g_i * XT + c * 128:g_i * XT + c * 128 + 128],
                                         qsb[:, :].rearrange("p (h n) -> p h n", h=4)[:, :, c * 128:(c + 1) * 128],
                                         start=True, stop=True),
                                t.matmul(pm3[0:64, 0:512],
                                         ktl[:, g_i * XT + c * 128 + 128:g_i * XT + c * 128 + WIN],
                                         qsb[:, :].rearrange("p (h n) -> p h n", h=4)[:, :, c * 128:(c + 1) * 128],
                                         start=True, stop=True)),
                                deps=[("dve", tB["qt2last"]),
                                      ("dve", max(tB["abev"], tB["pm3read"], tB["sctlast"]))])
                            tB["qsbreadpe"] = t_at

                            tB["abev"] = step("vector", lambda v: (
                                v.tensor_tensor(ab1[:, :].rearrange("p (h n) -> p h n", h=4),
                                                pAT[:, :].rearrange("p (h n) -> p h n", h=4),
                                                trimask[:, 0:128].unsqueeze(1).broadcast_to((128, 4, 128)),
                                                ALU.mult),
                                v.tensor_tensor(ab2[0:64, :].rearrange("p (h n) -> p h n", h=4),
                                                pm3[0:64, 0:512].rearrange("p (h n) -> p h n", h=4),
                                                trimask[0:64, 128:256].unsqueeze(1).broadcast_to((64, 4, 128)),
                                                ALU.mult)),
                                deps=[("pe", t_at)])
                            tB["pm3read"] = tB["abev"]

                            tB["o"] = step("tensor", lambda t, g_i=g_i, c=c: (
                                t.matmul(poB[:, :], vtk[:, vt(g_i, c):vt(g_i, c) + 128],
                                         p1t[:, :], start=True, stop=False),
                                t.matmul(poB[:, :], vtk[0:64, vt(g_i, c + 1):vt(g_i, c + 1) + 128],
                                         p2t[0:64, :], start=False, stop=False),
                                t.matmul(poB[:, :], stl[:, c * 128:(c + 1) * 128],
                                         qsb[:, :].rearrange("p (h n) -> p h n", h=4)[:, :, c * 128:(c + 1) * 128],
                                         start=False, stop=False),
                                t.matmul(poB[:, :], vtk[:, vt(g_i, c):vt(g_i, c) + 128],
                                         ab1[:, :], start=False, stop=False),
                                t.matmul(poB[:, :], vtk[0:64, vt(g_i, c + 1):vt(g_i, c + 1) + 128],
                                         ab2[0:64, :], start=False, stop=True)),
                                deps=[("act", tB["ptev"]), ("dve", tB["abev"]),
                                      ("act", tB["oev"])])
                            tB["qsbreadpe"] = tB["o"]
                            tB["stlread"] = tB["o"]

                            def oev(sc, g_i=g_i, c=c):
                                if om_dma:
                                    st_c.wait(sc, om_dma[-1])
                                sc.copy(omst[:, :], poB[:, :])
                            tB["oev"] = step("scalar", oev, deps=[("pe", tB["o"])])

                            def omdma(g, g_i=g_i, c=c):
                                C.wait(g, "act", tB["oev"])
                                om_dma.append(st_c.issue(g, lambda g=g, g_i=g_i, c=c: g.dma_start(
                                    omix_d[:, :].rearrange("p (h n) -> p h n", h=H)[:, 4 * g_i:4 * g_i + 4, c * 128:(c + 1) * 128],
                                    omst[:, :].rearrange("p (h n) -> p h n", h=4))))
                            block.gpsimd(omdma)

                        stageA(0)
                        for c in range(NCH):
                            if c + 1 < NCH:
                                stageA(c + 1)
                            stageB(c)

                    tap("omix0", omix_d[:, 0:TPC])
                    tap("omix4", omix_d[:, 4 * TPC:5 * TPC])
                    tap("omixall", omix_d[:, 0:H * TPC])
                    end_block(block)

        if "W" in phases:
            ret.close()
            with contextlib.ExitStack() as esW:
                sbW, psW = mk(esW)
                wobuf = sbW("wobuf", [128, 2 * 4 * 512], BF16)
                ombuf = sbW("ombuf", [128, 2 * 4 * TPC], BF16)
                sqsc = sbW("sqsc", [128, 512], BF16)
                rstb = sbW("rstb", [128, TPC], BF16)
                mrb = sbW("mrb", [128, TPC], BF16)
                rowsc2 = sbW("rowsc2", [1, 2048], F32)
                rowb = sbW("rowb", [1, 2048], BF16)
                pw = [psW(f"pw{i}", [128, 512], F32) for i in range(8)]

                with nc.Block() as block:
                    cur_block[0] = block
                    wodma = []
                    omdma2 = []
                    womm_pe = []
                    yev_dve = []

                    KG = 4
                    for mg in range(8):
                        for kg in range(NKC // KG):
                            i = mg * (NKC // KG) + kg

                            def wdma(g, i=i, mg=mg, kg=kg):
                                if i >= 2:
                                    g.wait_ge(C.sems["pe"], womm_pe[(i - 1) * KG - 1])
                                g.dma_start(
                                    wobuf[:, (i % 2) * KG * 512:(i % 2 + 1) * KG * 512].rearrange(
                                        "p (k n) -> p k n", k=KG),
                                    wo.rearrange("(kc p) n -> p kc n", p=128)[:, kg * KG:(kg + 1) * KG,
                                                                             mg * 512:(mg + 1) * 512])
                                wodma.append(C.inc("wdma0", g.last, 16) if False else None)
                            def wdma2(g, i=i, mg=mg, kg=kg):
                                if i >= 2:
                                    g.wait_ge(C.sems["pe"], womm_pe[(i - 1) * KG - 1])
                                wodma.append(st_w.issue(g, lambda g=g, mg=mg, kg=kg: g.dma_start(
                                    wobuf[:, (i % 2) * KG * 512:(i % 2 + 1) * KG * 512].rearrange(
                                        "p (k n) -> p k n", k=KG),
                                    wo.rearrange("(kc p) n -> p kc n", p=128)[:, kg * KG:(kg + 1) * KG,
                                                                             mg * 512:(mg + 1) * 512])))
                            block.gpsimd(wdma2)

                            def odma(s, i=i, kg=kg):
                                if i >= 2:
                                    s.wait_ge(C.sems["pe"], womm_pe[(i - 1) * KG - 1])
                                omdma2.append(st_a.issue(s, lambda s=s, kg=kg: s.dma_start(
                                    ombuf[:, (i % 2) * KG * TPC:(i % 2 + 1) * KG * TPC].rearrange(
                                        "p (k n) -> p k n", k=KG),
                                    omix_d[:, kg * KG * TPC:(kg + 1) * KG * TPC].rearrange(
                                        "p (k n) -> p k n", k=KG))))
                            block.sync(odma)

                            def womm(t, i=i, mg=mg, kg=kg):
                                st_w.wait(t, wodma[i])
                                st_a.wait(t, omdma2[i])
                                if kg == 0:
                                    if mg == 0:
                                        for nm in ("dve", "act", "pool"):
                                            if C.cnt[nm]:
                                                C.wait(t, nm)
                                    else:
                                        t.wait_ge(C.sems["dve"], yev_dve[mg - 1])
                                for k4 in range(KG):
                                    kcc = kg * KG + k4
                                    mm = None
                                    for m in range(4):
                                        for n in range(2):
                                            mm = t.matmul(
                                                pw[m * 2 + n][:, :],
                                                wobuf[:, (i % 2) * KG * 512 + k4 * 512 + m * 128:(i % 2) * KG * 512 + k4 * 512 + (m + 1) * 128],
                                                ombuf[:, (i % 2) * KG * TPC + k4 * TPC + n * 512:(i % 2) * KG * TPC + k4 * TPC + (n + 1) * 512],
                                                start=(kcc == 0), stop=(kcc == NKC - 1))
                                    womm_pe.append(C.inc("pe", mm))
                            block.tensor(womm)

                        def yev(v, mg=mg):
                            v.wait_ge(C.sems["pe"], womm_pe[-1])
                            for m in range(4):
                                kcg = mg * 4 + m
                                for n in range(2):
                                    v.tensor_tensor(
                                        xam[:, xm(kcg, n * 512):xm(kcg, (n + 1) * 512)],
                                        pw[m * 2 + n][:, :],
                                        xam[:, xm(kcg, n * 512):xm(kcg, (n + 1) * 512)],
                                        ALU.add)
                            yev_dve.append(C.inc("dve", v.engine_nop()))
                        block.vector(yev)

                    for kcg in range(NKC):
                        step("scalar", lambda sc, kcg=kcg: sc.activation(
                            sqsc[:, :], xam[:, xm(kcg, 0):xm(kcg, 512)], AF.Square))
                        step("tensor", lambda t, kcg=kcg: (
                            t.matmul(pw[4][0:1, :], onesb[:, 0:1], xam[:, xm(kcg, 0):xm(kcg, 512)],
                                     start=(kcg == 0), stop=(kcg == NKC - 1)),
                            t.matmul(pw[5][0:1, :], onesb[:, 0:1], xam[:, xm(kcg, 512):xm(kcg, 1024)],
                                     start=(kcg == 0), stop=(kcg == NKC - 1)),
                            t.matmul(pw[6][0:1, :], onesb[:, 0:1], sqsc[:, :],
                                     start=(kcg == 0), stop=(kcg == NKC - 1))))
                        step("scalar", lambda sc, kcg=kcg: sc.activation(
                            sqsc[:, :], xam[:, xm(kcg, 512):xm(kcg, 1024)], AF.Square))
                        step("tensor", lambda t, kcg=kcg: t.matmul(
                            pw[7][0:1, :], onesb[:, 0:1], sqsc[:, :],
                            start=(kcg == 0), stop=(kcg == NKC - 1)))

                    def rows1(v):
                        v.tensor_scalar(rowsc[0:1, 0:512], pw[4][0:1, :], 1.0 / DM, None, ALU.mult)
                        v.tensor_scalar(rowsc[0:1, 512:1024], pw[5][0:1, :], 1.0 / DM, None, ALU.mult)
                        v.tensor_scalar(rowsc2[0:1, 0:512], pw[6][0:1, :], 1.0 / DM, None, ALU.mult)
                        v.tensor_scalar(rowsc2[0:1, 512:1024], pw[7][0:1, :], 1.0 / DM, None, ALU.mult)
                    step("vector", rows1)
                    step("vector", lambda v: v.tensor_tensor(
                        rowsc2[0:1, 1024:2048], rowsc[0:1, 0:1024], rowsc[0:1, 0:1024], ALU.mult))
                    step("vector", lambda v: v.tensor_tensor(
                        rowsc2[0:1, 0:1024], rowsc2[0:1, 0:1024], rowsc2[0:1, 1024:2048], ALU.subtract))
                    step("vector", lambda v: v.tensor_scalar(
                        rowsc2[0:1, 0:1024], rowsc2[0:1, 0:1024], EPS, None, ALU.add))
                    step("scalar", lambda sc: sc.activation(
                        rowsc[0:1, 1024:2048], rowsc2[0:1, 0:1024], AF.Ln))
                    step("scalar", lambda sc: sc.activation(
                        rowsc[0:1, 1024:2048], rowsc[0:1, 1024:2048], AF.Exp, scale=-0.5))
                    step("vector", lambda v: v.tensor_tensor(
                        rowsc2[0:1, 1024:2048], rowsc[0:1, 0:1024],
                        rowsc[0:1, 1024:2048], ALU.mult))
                    step("vector", lambda v: (
                        v.tensor_copy(rowb[0:1, 0:1024], rowsc[0:1, 1024:2048]),
                        v.tensor_copy(rowb[0:1, 1024:2048], rowsc2[0:1, 1024:2048])))

                    for half in range(2):
                        step("tensor", lambda t, half=half: t.matmul(
                            pw[0][:, :], onesr[0:1, :], rowb[0:1, half * 512:(half + 1) * 512],
                            start=True, stop=True))
                        step("vector", lambda v, half=half: v.tensor_copy(
                            rstb[:, half * 512:(half + 1) * 512], pw[0][:, :]))
                        step("tensor", lambda t, half=half: t.matmul(
                            pw[1][:, :], onesr[0:1, :], rowb[0:1, 1024 + half * 512:1536 + half * 512],
                            start=True, stop=True))
                        step("vector", lambda v, half=half: v.tensor_copy(
                            mrb[:, half * 512:(half + 1) * 512], pw[1][:, :]))

                    def norm1a(v):
                        for kcg in range(NKC):
                            for half in range(2):
                                sl = xam[:, xm(kcg, half * 512):xm(kcg, (half + 1) * 512)]
                                v.tensor_tensor(sl, sl, rstb[:, half * 512:(half + 1) * 512], ALU.mult)
                    step("vector", norm1a)

                    def norm1b(v):
                        for kcg in range(NKC):
                            for half in range(2):
                                sl = xam[:, xm(kcg, half * 512):xm(kcg, (half + 1) * 512)]
                                v.tensor_tensor(sl, sl, mrb[:, half * 512:(half + 1) * 512], ALU.subtract)
                    step("vector", norm1b)

                    tap("x1_0", xam[:, xm(0, 0):xm(0, TPC)])
                    tap("x1_1", xam[:, xm(1, 0):xm(1, TPC)])
                    end_block(block)
        else:
            ret.close()

        if "F" in phases:
            with contextlib.ExitStack() as esF:
                sbF, psF = mk(esF)
                latb = sbF("latb", [128, NKC * NER], BF16)
                lbt1 = sbF("lbt1", [128, DM], BF16)
                lbt2 = sbF("lbt2", [96, DM], BF16)
                gwtb = sbF("gwtb", [128, NKC * NEXP], BF16)
                xexpb = sbF("xexpb", [28, NER], BF16)
                lsb = sbF("lsb", [28, TPC], BF16)
                ltk = sbF("ltk", [128, NEXP], F32)
                gsc = sbF("gsc", [128, 12], F32)
                gtk = sbF("gtk", [128, NEXP], BF16)
                gfm = sbF("gfm", [28, TPC], BF16)
                grep1 = sbF("grep1", [128, TPC], BF16)
                grep2 = sbF("grep2", [96, TPC], BF16)
                htl1 = sbF("htl1", [128, TPC], BF16)
                htl2 = sbF("htl2", [96, TPC], BF16)
                bwbuf = sbF("bwbuf", [128, 2 * 4 * 384], BF16)
                y2b = sbF("y2b", [128, NKC * TPC], BF16)
                sqsc2 = sbF("sqsc2", [128, 512], BF16)
                mcol = sbF("mcol", [128, 4], F32)
                xstg = sbF("xstg", [128, 2 * 1024], F32)
                rowsc2 = sbF("rowscF", [1, 1024], F32)
                pf = [psF(f"pf{i}", [128, 512], F32) for i in range(7)]
                ptbF = psF("ptbF", [128, 1024], BF16)

                with nc.Block() as block:
                    cur_block[0] = block

                    sm_toks = []

                    def ldsm(g):
                        for dst, src in ((latb[:, :].rearrange("p (kc n) -> p kc n", kc=NKC),
                                          lat.rearrange("(kc p) n -> p kc n", p=128)),
                                         (lbt1[:, :], lbt[0:128, :]),
                                         (lbt2[:, :], lbt[128:224, :]),
                                         (gwtb[:, :].rearrange("p (kc n) -> p kc n", kc=NKC),
                                          gwt.rearrange("(kc p) n -> p kc n", p=128)),
                                         (xexpb[:, :], xexph)):
                            sm_toks.append(st_g.issue(g, lambda g=g, dst=dst, src=src:
                                                      g.dma_start(dst, src)))
                    block.gpsimd(ldsm)

                    def lg(t):
                        st_g.wait_all(t)
                        for n in range(2):
                            for kcc in range(NKC):
                                t.matmul(pf[n][0:28, :],
                                         gwtb[:, kcc * NEXP:(kcc + 1) * NEXP],
                                         xam[:, xm(kcc, n * 512):xm(kcc, (n + 1) * 512)],
                                         start=(kcc == 0), stop=(kcc == NKC - 1))
                    step("tensor", lg)
                    step("vector", lambda v: (
                        v.tensor_copy(lsb[:, 0:512], pf[0][0:28, :]),
                        v.tensor_copy(lsb[:, 512:1024], pf[1][0:28, :])))

                    for tcj in range(NCH):
                        step("tensor", lambda t, tcj=tcj: t.transpose(
                            ptbF[0:128, 0:28], lsb[:, tcj * 128:(tcj + 1) * 128], ident[0:28, 0:28]))

                        step("vector", lambda v: v.tensor_copy(ltk[:, :], ptbF[0:128, 0:28]))
                        step("vector", lambda v: v.tensor_reduce(
                            gsc[:, 0:1], ltk[:, :], mybir.AxisListType.X, ALU.max))
                        step("vector", lambda v: v.tensor_tensor(
                            gtk[:, :], ltk[:, :],
                            gsc[:, 0:1].broadcast_to((128, NEXP)), ALU.is_ge))
                        step("vector", lambda v: v.scalar_tensor_tensor(
                            ltk[:, :], gtk[:, :], -1e9, ltk[:, :], ALU.mult, ALU.add))
                        step("vector", lambda v: v.tensor_reduce(
                            gsc[:, 1:2], ltk[:, :], mybir.AxisListType.X, ALU.max))
                        step("vector", lambda v: v.tensor_tensor(
                            gsc[:, 2:3], gsc[:, 1:2], gsc[:, 0:1], ALU.subtract))
                        step("scalar", lambda sc: sc.activation(gsc[:, 3:4], gsc[:, 2:3], AF.Exp))
                        step("vector", lambda v: v.tensor_scalar(
                            gsc[:, 4:5], gsc[:, 3:4], 1.0, None, ALU.add))
                        step("vector", lambda v: v.reciprocal(gsc[:, 5:6], gsc[:, 4:5]))
                        step("vector", lambda v: (
                            v.tensor_tensor(gsc[:, 6:7], gsc[:, 5:6], gsc[:, 3:4], ALU.mult),
                            v.tensor_tensor(ltk[:, :], ltk[:, :],
                                            gsc[:, 1:2].broadcast_to((128, NEXP)), ALU.is_ge)))
                        step("vector", lambda v: (
                            v.tensor_tensor(gtk[:, :], gtk[:, :],
                                            gsc[:, 5:6].broadcast_to((128, NEXP)), ALU.mult),
                            v.scalar_tensor_tensor(ltk[:, :], ltk[:, :], 1.0,
                                                   gsc[:, 6:7].broadcast_to((128, NEXP)),
                                                   ALU.mult, ALU.mult)))
                        step("vector", lambda v: v.tensor_tensor(
                            gtk[:, :], gtk[:, :], ltk[:, :], ALU.add))
                        step("tensor", lambda t: t.transpose(
                            ptbF[0:28, 0:128], gtk[:, :], ident[:, :]))
                        step("vector", lambda v, tcj=tcj: v.tensor_copy(
                            gfm[:, tcj * 128:(tcj + 1) * 128], ptbF[0:28, 0:128]))

                    for n in range(2):
                        step("tensor", lambda t, n=n: (
                            t.matmul(pf[3][:, :], xexpb[:, 0:128], gfm[:, n * 512:(n + 1) * 512],
                                     start=True, stop=True),
                            t.matmul(pf[4][0:96, :], xexpb[:, 128:224], gfm[:, n * 512:(n + 1) * 512],
                                     start=True, stop=True)))
                        step("vector", lambda v, n=n: (
                            v.tensor_copy(grep1[:, n * 512:(n + 1) * 512], pf[3][:, :]),
                            v.tensor_copy(grep2[:, n * 512:(n + 1) * 512], pf[4][0:96, :])))

                    for n in range(2):
                        def hmm(t, n=n):
                            for kcc in range(NKC):
                                t.matmul(pf[3][:, :], latb[:, kcc * NER:kcc * NER + 128],
                                         xam[:, xm(kcc, n * 512):xm(kcc, (n + 1) * 512)],
                                         start=(kcc == 0), stop=(kcc == NKC - 1))
                                t.matmul(pf[4][0:96, :], latb[:, kcc * NER + 128:kcc * NER + 224],
                                         xam[:, xm(kcc, n * 512):xm(kcc, (n + 1) * 512)],
                                         start=(kcc == 0), stop=(kcc == NKC - 1))
                        step("tensor", hmm)
                        step("vector", lambda v, n=n: (
                            v.tensor_tensor(htl1[:, n * 512:(n + 1) * 512], pf[3][:, :],
                                            grep1[:, n * 512:(n + 1) * 512], ALU.mult),
                            v.tensor_tensor(htl2[:, n * 512:(n + 1) * 512], pf[4][0:96, :],
                                            grep2[:, n * 512:(n + 1) * 512], ALU.mult)))

                    bw_dma = []
                    bmm_pe = []
                    y2_dve = []
                    mgs = []
                    m0 = 0
                    while m0 < NKC:
                        g_sz = min(3, NKC - m0)
                        mgs.append((m0, g_sz))
                        m0 += g_sz
                    KG = 4
                    gi = 0
                    for (m0, g_sz) in mgs:
                        for kg in range(NKC // KG):
                            i = gi * (NKC // KG) + kg

                            def bdma(g, i=i, m0=m0, g_sz=g_sz, kg=kg):
                                if i >= 2:
                                    g.wait_ge(C.sems["pe"], bmm_pe[(i - 1) * KG - 1])
                                bw_dma.append(st_w.issue(g, lambda g=g, i=i, m0=m0, g_sz=g_sz, kg=kg: g.dma_start(
                                    bwbuf[:, (i % 2) * KG * 384:(i % 2) * KG * 384 + KG * 384].rearrange(
                                        "p (k n) -> p k n", k=KG)[:, :, 0:g_sz * 128],
                                    bwt.rearrange("(kc p) n -> p kc n", p=128)[:, kg * KG:(kg + 1) * KG,
                                                                              m0 * 128:(m0 + g_sz) * 128])))
                            block.gpsimd(bdma)

                            def bmm(t, i=i, g_sz=g_sz, kg=kg, gi=gi):
                                st_w.wait(t, bw_dma[i])
                                if kg == 0:
                                    if gi == 0:
                                        C.wait(t, "dve")
                                    else:
                                        t.wait_ge(C.sems["dve"], y2_dve[gi - 1])
                                for k4 in range(KG):
                                    kcc = kg * KG + k4
                                    mm = None
                                    for m in range(g_sz):
                                        for n in range(2):
                                            mm = t.matmul(
                                                pf[m * 2 + n][:, :],
                                                bwbuf[:, (i % 2) * KG * 384 + k4 * 384 + m * 128:(i % 2) * KG * 384 + k4 * 384 + (m + 1) * 128],
                                                xam[:, xm(kcc, n * 512):xm(kcc, (n + 1) * 512)],
                                                start=(kcc == 0), stop=False)
                                    bmm_pe.append(C.inc("pe", mm))
                            block.tensor(bmm)

                        def lmm(t, m0=m0, g_sz=g_sz):
                            mm = None
                            for m in range(g_sz):
                                for n in range(2):
                                    t.matmul(pf[m * 2 + n][:, :],
                                             lbt1[:, (m0 + m) * 128:(m0 + m + 1) * 128],
                                             htl1[:, n * 512:(n + 1) * 512],
                                             start=False, stop=False)
                                    mm = t.matmul(pf[m * 2 + n][:, :],
                                                  lbt2[:, (m0 + m) * 128:(m0 + m + 1) * 128],
                                                  htl2[:, n * 512:(n + 1) * 512],
                                                  start=False, stop=True)
                            bmm_pe[-1] = C.inc("pe", mm)
                        block.tensor(lmm)

                        def y2ev(v, m0=m0, g_sz=g_sz):
                            C.wait(v, "pe")
                            for m in range(g_sz):
                                kcg = m0 + m
                                for n in range(2):
                                    v.tensor_tensor(
                                        y2b[:, kcg * TPC + n * 512:kcg * TPC + (n + 1) * 512],
                                        pf[m * 2 + n][:, :],
                                        xam[:, xm(kcg, n * 512):xm(kcg, (n + 1) * 512)],
                                        ALU.add)
                            y2_dve.append(C.inc("dve", v.engine_nop()))
                        block.vector(y2ev)
                        gi += 1

                    for kcg in range(NKC):
                        step("scalar", lambda sc, kcg=kcg: sc.activation(
                            sqsc2[:, :], y2b[:, kcg * TPC:kcg * TPC + 512], AF.Square))
                        step("tensor", lambda t, kcg=kcg: (
                            t.matmul(pf[3][0:1, :], onesb[:, 0:1], y2b[:, kcg * TPC:kcg * TPC + 512],
                                     start=(kcg == 0), stop=(kcg == NKC - 1)),
                            t.matmul(pf[4][0:1, :], onesb[:, 0:1], y2b[:, kcg * TPC + 512:(kcg + 1) * TPC],
                                     start=(kcg == 0), stop=(kcg == NKC - 1)),
                            t.matmul(pf[5][0:1, :], onesb[:, 0:1], sqsc2[:, :],
                                     start=(kcg == 0), stop=(kcg == NKC - 1))))
                        step("scalar", lambda sc, kcg=kcg: sc.activation(
                            sqsc2[:, :], y2b[:, kcg * TPC + 512:(kcg + 1) * TPC], AF.Square))
                        step("tensor", lambda t, kcg=kcg: t.matmul(
                            pf[6][0:1, :], onesb[:, 0:1], sqsc2[:, :],
                            start=(kcg == 0), stop=(kcg == NKC - 1)))

                    def rows2(v):
                        v.tensor_scalar(rowsc[0:1, 0:512], pf[3][0:1, :], 1.0 / DM, None, ALU.mult)
                        v.tensor_scalar(rowsc[0:1, 512:1024], pf[4][0:1, :], 1.0 / DM, None, ALU.mult)
                        v.tensor_scalar(rowsc2[0:1, 0:512], pf[5][0:1, :], 1.0 / DM, None, ALU.mult)
                        v.tensor_scalar(rowsc2[0:1, 512:1024], pf[6][0:1, :], 1.0 / DM, None, ALU.mult)
                    step("vector", rows2)
                    step("vector", lambda v: v.tensor_tensor(
                        xstg[0:1, 0:1024], rowsc[0:1, 0:1024], rowsc[0:1, 0:1024], ALU.mult))
                    step("vector", lambda v: v.tensor_tensor(
                        rowsc2[0:1, 0:1024], rowsc2[0:1, 0:1024], xstg[0:1, 0:1024], ALU.subtract))
                    step("vector", lambda v: v.tensor_scalar(
                        rowsc2[0:1, 0:1024], rowsc2[0:1, 0:1024], EPS, None, ALU.add))
                    step("scalar", lambda sc: sc.activation(
                        rowsc[0:1, 1024:2048], rowsc2[0:1, 0:1024], AF.Ln))
                    step("scalar", lambda sc: sc.activation(
                        rowsc[0:1, 1024:2048], rowsc[0:1, 1024:2048], AF.Exp, scale=-0.5))

                    ost = []
                    for tt in range(NCH):
                        step("tensor", lambda t, tt=tt: (
                            t.transpose(pf[3][0:128, 0:1],
                                        rowsc[0:1, tt * 128:(tt + 1) * 128], identf[0:1, 0:1]),
                            t.transpose(pf[4][0:128, 0:1],
                                        rowsc[0:1, 1024 + tt * 128:1024 + (tt + 1) * 128],
                                        identf[0:1, 0:1])))
                        step("vector", lambda v: (
                            v.tensor_copy(mcol[:, 0:1], pf[3][0:128, 0:1]),
                            v.tensor_copy(mcol[:, 1:2], pf[4][0:128, 0:1])))
                        step("scalar", lambda sc: None)

                        for qr in range(4):
                            def trs(t, tt=tt, qr=qr):
                                for k8 in range(8):
                                    kcg = qr * 8 + k8
                                    t.transpose(ptbF[:, k8 * 128:(k8 + 1) * 128],
                                                y2b[:, kcg * TPC + tt * 128:kcg * TPC + (tt + 1) * 128],
                                                ident[:, :])
                            step("tensor", trs)

                            def nrm(v, tt=tt, qr=qr):
                                if len(ost) >= 2:
                                    st_o.wait(v, ost[-2])
                                for k8 in range(8):
                                    v.tensor_scalar(
                                        xstg[:, (qr % 2) * 1024 + k8 * 128:(qr % 2) * 1024 + (k8 + 1) * 128],
                                        ptbF[:, k8 * 128:(k8 + 1) * 128],
                                        mcol[:, 0:1], mcol[:, 1:2],
                                        ALU.subtract, ALU.mult)
                            step("vector", nrm)

                            def outdma(s, tt=tt, qr=qr):
                                for nm in CSEMS:
                                    C.wait(s, nm)
                                ost.append(st_o.issue(s, lambda s=s, tt=tt, qr=qr: s.dma_start(
                                    out[tt * 128:(tt + 1) * 128, qr * 1024:(qr + 1) * 1024],
                                    xstg[:, (qr % 2) * 1024:(qr % 2 + 1) * 1024])))
                            block.sync(outdma)

                    end_block(block)
        else:
            with contextlib.ExitStack() as esZ:
                sbZ, _ = mk(esZ)
                zb = sbZ("zb", [128, 2048], F32)
                with nc.Block() as block:
                    cur_block[0] = block
                    step("vector", lambda v: v.memset(zb[:, :], 0))

                    def zdma(s):
                        C.wait(s, "dve")
                        for tt in range(NCH):
                            for half in range(2):
                                st_o.issue(s, lambda s=s, tt=tt, half=half: s.dma_start(
                                    out[tt * 128:(tt + 1) * 128, half * 2048:(half + 1) * 2048],
                                    zb[:, :]))
                        st_o.wait_all(s)
                    block.sync(zdma)
                    end_block(block)

    return nc, tap_d



def _host_constants(core):
    bf = ml_dtypes.bfloat16
    ident = np.eye(128, dtype=np.float32).astype(bf)
    renc = np.zeros((128, 128), dtype=np.float32)
    for dp in range(64):
        renc[dp + 64, dp] = -1.0
        renc[dp, dp + 64] = 1.0
    renc = renc.astype(bf)
    start = core * TPC - HALO
    pos = np.arange(start, start + XT, dtype=np.float64)
    inv = 1.0 / (10000.0 ** (2.0 * np.arange(64, dtype=np.float64) / 128.0))
    fr = np.concatenate([inv, inv])[:, None] * pos[None, :]
    cost = np.cos(fr).astype(np.float32).astype(bf)
    sint = np.sin(fr).astype(np.float32).astype(bf)
    i = np.arange(128)[:, None]
    j = np.arange(WIN)[None, :]
    m = ((j >= i) & (j <= i + 64)).astype(np.float32)
    mf = m * (j >= 64)
    if core == 0:
        maskc = np.concatenate([m, mf], axis=1).astype(bf)
    else:
        maskc = np.concatenate([m, m], axis=1).astype(bf)
    it = np.arange(128)[None, :]
    je = np.arange(128)[:, None]
    t1 = 0.5 * ((je >= 64) & (je - 64 <= it)).astype(np.float32)
    t2 = 0.5 * ((je + 64 <= it)).astype(np.float32)
    trimv = np.concatenate([t1, t2], axis=1).astype(bf)
    ones = np.zeros((128, 2), dtype=np.float32)
    ones[:, 0] = 1.0
    onesr = np.ones((1, 128), dtype=np.float32).astype(bf)
    onesbf = np.ones((128, 1), dtype=np.float32)
    onesrf = np.ones((1, 128), dtype=np.float32)
    cm = np.zeros((128, 16), dtype=np.float32)
    cm[:, 0:8] = (np.arange(8)[None, :] < core).astype(np.float32)
    cm[:, 8:16] = 1.0 - cm[:, 0:8]
    xexp = np.zeros((NEXP, NER), dtype=np.float32)
    for e in range(NEXP):
        xexp[e, e * 8:(e + 1) * 8] = 2.0
    return dict(identh=ident, identfh=np.eye(128, dtype=np.float32), rench=renc,
                cosh=cost, sinh=sint, maskh=maskc, trih=trimv,
                onesh=ones.astype(bf), onesrh=onesr, onesbfh=onesbf,
                onesrfh=onesrf, cmaskh=cm, xexph=xexp.astype(bf))


def _host_inputs(inputs):
    x = np.asarray(inputs["L0_attn_hidden_states"]).reshape(NTOK, DM)
    xpad = np.concatenate([np.zeros((HALO, DM), np.float32), x], axis=0)
    base = {
        "wq": np.ascontiguousarray(np.asarray(inputs["L0_attn_Wq"])),
        "wk": np.ascontiguousarray(np.asarray(inputs["L0_attn_Wk"])),
        "wv": np.ascontiguousarray(np.asarray(inputs["L0_attn_Wv"])),
        "wo": np.ascontiguousarray(np.asarray(inputs["L0_attn_Wo"])),
        "bwt": np.ascontiguousarray(np.asarray(inputs["L0_ffn_base_W"]).T),
        "gwt": np.ascontiguousarray(np.asarray(inputs["L0_ffn_gate_W"]).T),
        "lat": np.ascontiguousarray(
            np.asarray(inputs["L0_ffn_lora_A"]).transpose(2, 0, 1).reshape(DM, NER)),
        "lbt": np.ascontiguousarray(
            np.asarray(inputs["L0_ffn_lora_B"]).transpose(0, 2, 1).reshape(NER, DM)),
    }
    maps = []
    for c in range(NCORE):
        m = dict(base)
        m["x"] = np.ascontiguousarray(xpad[c * TPC:c * TPC + XT])
        m.update(_host_constants(c))
        maps.append(m)
    return maps


def kernel(**inputs):
    maps = _host_inputs(inputs)
    nc, _ = build_program()
    res = run_bass_kernel_spmd(nc, maps, core_ids=list(range(NCORE)))
    outs = [res.results[c]["out"] for c in range(NCORE)]
    return np.concatenate(outs, axis=0).reshape(1, NTOK, DM)


# version 11 (final)
# speedup vs baseline: 1.0105x; 1.0105x over previous
import contextlib
import numpy as np
import ml_dtypes
import concourse.bass as bass
import concourse.mybir as mybir
from concourse.bass_utils import run_bass_kernel_spmd

F32 = mybir.dt.float32
BF16 = mybir.dt.bfloat16
AF = mybir.ActivationFunctionType
ALU = mybir.AluOpType

NCORE = 8
DM = 4096
NTOK = 8192
TPC = 1024
HALO = 64
XT = TPC + HALO
NXT = 9
H = 32
KVH = 8
NKC = 32
CHUNK = 128
NCH = 8
WIN = 192
NEXP = 28
NER = 224
GINV = 1.0 / 16.0
ISQD = float(1.0 / np.sqrt(128.0))
EPS = 1e-5

TAPS = {}
ENG2SEM = {"tensor": "pe", "vector": "dve", "scalar": "act", "gpsimd": "pool"}
CSEMS = ("pe", "dve", "act", "pool")


def xm(kc, col=0):
    return kc * TPC + col


def build_program(taps=None, phases="ACBWF", a_limit=99, c_limit=99):
    import os
    ASTG = int(os.environ.get("ASTG", "9"))
    taps = dict(taps if taps is not None else TAPS)
    nc = bass.Bass("TRN2", target_bir_lowering=False)

    x_in = nc.dram_tensor("x", [XT, DM], F32, kind="ExternalInput").ap()
    wq = nc.dram_tensor("wq", [DM, DM], F32, kind="ExternalInput").ap()
    wk = nc.dram_tensor("wk", [DM, KVH * 128], F32, kind="ExternalInput").ap()
    wv = nc.dram_tensor("wv", [DM, KVH * 128], F32, kind="ExternalInput").ap()
    wo = nc.dram_tensor("wo", [DM, DM], F32, kind="ExternalInput").ap()
    bwt = nc.dram_tensor("bwt", [DM, DM], F32, kind="ExternalInput").ap()
    gwt = nc.dram_tensor("gwt", [DM, NEXP], F32, kind="ExternalInput").ap()
    lat = nc.dram_tensor("lat", [DM, NER], F32, kind="ExternalInput").ap()
    lbt = nc.dram_tensor("lbt", [NER, DM], F32, kind="ExternalInput").ap()
    identh = nc.dram_tensor("identh", [128, 128], BF16, kind="ExternalInput").ap()
    identfh = nc.dram_tensor("identfh", [128, 128], F32, kind="ExternalInput").ap()
    rench = nc.dram_tensor("rench", [128, 128], BF16, kind="ExternalInput").ap()
    cosh = nc.dram_tensor("cosh", [128, XT], BF16, kind="ExternalInput").ap()
    sinh = nc.dram_tensor("sinh", [128, XT], BF16, kind="ExternalInput").ap()
    maskh = nc.dram_tensor("maskh", [128, 2 * WIN], BF16, kind="ExternalInput").ap()
    trih = nc.dram_tensor("trih", [128, 256], BF16, kind="ExternalInput").ap()
    onesh = nc.dram_tensor("onesh", [128, 2], BF16, kind="ExternalInput").ap()
    onesrh = nc.dram_tensor("onesrh", [1, 128], BF16, kind="ExternalInput").ap()
    onesbfh = nc.dram_tensor("onesbfh", [128, 1], F32, kind="ExternalInput").ap()
    onesrfh = nc.dram_tensor("onesrfh", [1, 128], F32, kind="ExternalInput").ap()
    cmaskh = nc.dram_tensor("cmaskh", [128, 16], F32, kind="ExternalInput").ap()
    xexph = nc.dram_tensor("xexph", [NEXP, NER], BF16, kind="ExternalInput").ap()
    out = nc.dram_tensor("out", [TPC, DM], F32, kind="ExternalOutput").ap()

    omix_d = nc.dram_tensor("omix_d", [128, H * TPC], BF16)
    cc_in = nc.dram_tensor("cc_in", [KVH * 129, 128], F32)
    cc_out = nc.dram_tensor("cc_out", [NCORE * KVH * 129, 128], F32)

    tap_d = {}
    for name, (shape, dt) in taps.items():
        tap_d[name] = nc.dram_tensor("tap_" + name, list(shape), dt, kind="ExternalOutput").ap()
    tap_jobs = []

    def tap(name, ap):
        if name in tap_d:
            tap_jobs.append((name, ap))

    class Ctr:
        def __init__(self):
            self.sems = {}
            self.cnt = {}
            self.lastw = {}

        def add(self, es, *names):
            for n in names:
                self.sems[n] = es.enter_context(nc.semaphore(n))
                self.cnt[n] = 0

        def inc(self, name, instr, k=1):
            instr.then_inc(self.sems[name], k)
            self.cnt[name] += k
            return self.cnt[name]

        def wait(self, eng, name, v=None, key=None):
            tv = self.cnt[name] if v is None else v
            if tv <= 0:
                return
            if key is not None and not os.environ.get("NOTRACK"):
                if self.lastw.get((key, name), -1) >= tv:
                    return
                self.lastw[(key, name)] = tv
            eng.wait_ge(self.sems[name], tv)

    C = Ctr()
    glob = contextlib.ExitStack()

    def mk(es):
        return (lambda nm, shape, dt: es.enter_context(nc.sbuf_tensor(nm, shape, dt)),
                lambda nm, shape, dt: es.enter_context(nc.psum_tensor(nm, shape, dt)))

    cur_block = [None]

    class _Rec:

        def __init__(self, eng):
            self._e = eng
            self.last = None

        def __getattr__(self, name):
            f = getattr(self._e, name)
            if not callable(f):
                return f

            def wrap(*a, **kw):
                r = f(*a, **kw)
                if hasattr(r, "then_inc"):
                    self.last = r
                return r
            return wrap

    def step(eng, fn, dma=(), deps=None):
        sem = ENG2SEM[eng]

        def body(e):
            if deps is None:
                for w in CSEMS:
                    C.wait(e, w, key=eng)
            else:
                C.wait(e, sem, key=eng)
                for wsem, wval in deps:
                    C.wait(e, wsem, wval, key=eng)
            for wsem, wval in dma:
                e.wait_ge(C.sems[wsem], wval)
            rec = _Rec(e)
            fn(rec)
            if rec.last is None:
                rec.last = e.nop() if eng in ("tensor", "scalar") else e.engine_nop()
            C.inc(sem, rec.last)
        getattr(cur_block[0], eng)(body)
        return C.cnt[sem]

    with glob:
        sbg, _psg = mk(glob)
        C.add(glob, "pe", "dve", "act", "pool", "cc")

        class Stream:
            def __init__(self, base, n):
                self.names = [f"{base}{i}" for i in range(n)]
                C.add(glob, *self.names)
                self.n = n
                self.k = 0

            def issue(self, eng, mk_instr):
                nm = self.names[self.k % self.n]
                if self.k >= self.n:
                    eng.wait_ge(C.sems[nm], 16 * (self.k // self.n))
                C.inc(nm, mk_instr(), 16)
                tok = (nm, C.cnt[nm])
                self.k += 1
                return tok

            def wait(self, eng, tok, key=None):
                C.wait(eng, tok[0], tok[1], key=key)

            def wait_all(self, eng, key=None):
                for nm in self.names:
                    C.wait(eng, nm, key=key)

        st_w = Stream("wdma", 4)
        st_g = Stream("gdma", 2)
        st_a = Stream("adma", 2)
        st_o = Stream("odma", 2)
        st_c = Stream("cdma", 2)
        streams = (st_w, st_g, st_a, st_o, st_c)

        ident = sbg("ident", [128, 128], BF16)
        identf = sbg("identf", [128, 128], F32)
        renc = sbg("renc", [128, 128], BF16)
        costab = sbg("costab", [128, XT], BF16)
        sintab = sbg("sintab", [128, XT], BF16)
        maskswa = sbg("maskswa", [128, 2 * WIN], BF16)
        trimask = sbg("trimask", [128, 256], BF16)
        onesb = sbg("onesb", [128, 2], BF16)
        onesr = sbg("onesr", [1, 128], BF16)
        onesbf = sbg("onesbf", [128, 1], F32)
        onesrf = sbg("onesrf", [1, 128], F32)
        cmask = sbg("cmask", [128, 16], F32)
        xam = sbg("xam", [128, NKC * TPC], BF16)
        rowsc = sbg("rowsc", [1, 2048], F32)
        colsc = sbg("colsc", [128, 16], F32)

        def end_block(block):
            jobs = list(tap_jobs)
            tap_jobs.clear()

            @block.sync
            def _(s):
                for nm in CSEMS:
                    if C.cnt[nm]:
                        C.wait(s, nm)
                for st in streams:
                    st.wait_all(s)
                if C.cnt["cc"]:
                    C.wait(s, "cc")
                for name, ap in jobs:
                    st_o.issue(s, lambda s=s, name=name, ap=ap: s.dma_start(tap_d[name], ap))
                st_o.wait_all(s)

            @block.gpsimd
            def _(g):
                for nm in CSEMS + ("cc",):
                    if C.cnt[nm]:
                        C.wait(g, nm)
                for st in streams:
                    st.wait_all(g)

        es_halo = contextlib.ExitStack()
        sbh, _ = mk(es_halo)
        xah = sbh("xah", [128, NKC * HALO], BF16)

        def pin_act_table(block):
            return

        with contextlib.ExitStack() as es0:
            sb0, ps0 = mk(es0)
            xtok = sb0("xtok", [128, 2 * DM], BF16)
            ptb0 = [ps0(f"ptb0_{i}", [128, 1024], BF16) for i in range(2)]

            with nc.Block() as block:
                cur_block[0] = block
                pin_act_table(block)

                const_toks = []

                @block.sync
                def _(s):
                    for dst, src in ((ident, identh), (identf, identfh), (renc, rench),
                                     (costab, cosh), (sintab, sinh), (maskswa, maskh),
                                     (trimask, trih), (onesb, onesh), (onesr, onesrh),
                                     (onesbf, onesbfh), (onesrf, onesrfh),
                                     (cmask, cmaskh)):
                        const_toks.append(st_a.issue(s, lambda s=s, dst=dst, src=src:
                                                     s.dma_start(dst[:, :], src)))

                xtok_done = []
                tile_tr_done = []
                for t_i in range(NXT):
                    rows = 128 if t_i < 8 else 64

                    def dma_body(g, t_i=t_i, rows=rows):
                        if t_i >= 2:
                            g.wait_ge(C.sems["pe"], tile_tr_done[t_i - 2])
                        xtok_done.append(st_g.issue(g, lambda g=g, t_i=t_i, rows=rows: g.dma_start(
                            xtok[0:rows, bass.ts(t_i % 2, DM)],
                            x_in[t_i * 128:t_i * 128 + rows, :])))
                    block.gpsimd(dma_body)

                    for grp in range(4):
                        def tr_body(t, t_i=t_i, rows=rows, grp=grp):
                            if grp == 0:
                                st_a.wait_all(t)
                                st_g.wait(t, xtok_done[t_i])
                            for kc8 in range(8):
                                kc = grp * 8 + kc8
                                t.transpose(ptb0[t_i % 2][:, kc8 * 128:kc8 * 128 + rows],
                                            xtok[0:rows, (t_i % 2) * DM + kc * 128:(t_i % 2) * DM + (kc + 1) * 128],
                                            ident[0:rows, 0:rows])
                        tr_pe = step("tensor", tr_body)

                        def ev_body(v, t_i=t_i, rows=rows, grp=grp):
                            for kc8 in range(8):
                                kc = grp * 8 + kc8
                                srcp = ptb0[t_i % 2][:, kc8 * 128:kc8 * 128 + rows]
                                if t_i == 0:
                                    v.tensor_copy(xah[:, kc * HALO:(kc + 1) * HALO], srcp[:, 0:64])
                                    v.tensor_copy(xam[:, xm(kc, 0):xm(kc, 64)], srcp[:, 64:128])
                                else:
                                    v.tensor_copy(xam[:, xm(kc, t_i * 128 - 64):xm(kc, t_i * 128 - 64 + rows)],
                                                  srcp)
                        step("vector", ev_body)
                    tile_tr_done.append(tr_pe)

                tap("xam0", xam[:, 0:TPC])
                tap("xah0", xah[:, 0:HALO])
                end_block(block)

        ret = contextlib.ExitStack()
        sbr = lambda nm, shape, dt: ret.enter_context(
            nc.sbuf_tensor(nm, shape, dt, side="right"))
        sk = sbr("skt", [128, KVH * XT], BF16)
        vtk = sbr("vtk", [128, KVH * NXT * 128], BF16)
        ktl = sbr("ktl", [128, KVH * XT], BF16)
        vtk2 = sbr("vtk2", [128, KVH * NCH * 128], BF16)
        epp = sbr("epp", [128, KVH * TPC], BF16)
        ebc = sbr("ebc", [128, KVH * NCH], F32)
        ecm = sbr("ecm", [128, KVH * NCH], F32)
        fhat = sbr("fhat", [128, KVH * NCH], F32)
        tseg = sbr("tseg", [128, KVH * 128], F32)
        bsum = sbr("bsum", [128, KVH], F32)
        ebsg = sbr("ebsg", [128, KVH], F32)
        ebsgr = sbr("ebsgr", [KVH, 128], F32)

        def vt(h, t):
            return (h * NXT + t) * 128

        if "A" in phases:
            with contextlib.ExitStack() as esA:
                sbA, psA = mk(esA)
                wbufA = sbA("wbufA", [128, 2 * 16 * 128], BF16)
                kf = sbA("kf", [128, TPC], F32)
                kb = sbA("kb", [128, XT], BF16)
                eqf = sbA("eqf", [128, XT], F32)
                gsab = sbA("gsab", [128, 512], F32)
                gsmn = sbA("gsmn", [128, 512], F32)
                cumv = sbA("cumv", [128, CHUNK], F32)
                ksc = sbA("ksc", [128, CHUNK], F32)
                ene = sbA("ene", [128, CHUNK], F32)
                vb = sbA("vb", [128, XT], BF16)
                ktok = sbA("ktokA", [128, 128], BF16)
                sct = sbA("sctA", [128, 128], F32)
                ru = sbA("ruA", [128, 512], F32)
                rt = sbA("rtA", [128, 512], F32)
                pga = [psA(f"pga{i}", [128, 512], F32) for i in range(2)]
                pn64 = psA("pn64", [128, 64], F32)
                pmisc = psA("pmiscA", [128, 512], F32)
                pm2 = psA("pm2A", [128, 512], F32)
                ptbA = psA("ptbA", [128, 1024], BF16)
                ptbA2 = psA("ptbA2", [128, 1024], BF16)

                with nc.Block() as block:
                    cur_block[0] = block
                    wdone = []
                    wrel = [None] * 32

                    def wdma(g, s):
                        src = wk if s < 16 else wv
                        hh = (s % 16) // 2
                        hf = s % 2
                        if s >= 2:
                            g.wait_ge(C.sems["pe"], wrel[s - 2])
                        wdone.append(st_w.issue(g, lambda g=g, s=s, src=src, hf=hf, hh=hh: g.dma_start(
                            wbufA[:, bass.ts(s % 2, 16 * 128)].rearrange(
                                "p (kc n) -> p kc n", kc=16),
                            src.rearrange("(kc p) n -> p kc n", p=128)[:, 16 * hf:16 * hf + 16,
                                                                      hh * 128:(hh + 1) * 128])))

                    block.gpsimd(lambda g: (wdma(g, 0), wdma(g, 1)))

                    def z0(v):
                        for h in range(KVH):
                            v.memset(ktl[:, h * XT:h * XT + HALO], 0)
                            v.memset(vtk[64:128, vt(h, 8):vt(h, 8) + 128], 0)
                        v.memset(tseg[:, :], 0)
                        v.memset(bsum[:, :], 0)
                    step("vector", z0)

                    tA = dict(eqkb=[0] * 16, kfev=[0] * 16, vev=[0] * 16,
                              abslast=[0] * 16, ln=[0] * 16, rtlast=[0] * 16,
                              gssub=0, chunk1=0, chunk2=0, ktl=[0] * 8, ktllast=[0] * 8,
                              ptbacopy=[0] * 16, vtk2last=[0] * 8)
                    for i in range(min(16, a_limit)):
                        isK = i < 8
                        h = i % 8

                        def gemm(t, i=i):
                            for hf in range(2):
                                st_w.wait(t, wdone[2 * i + hf])
                                for k16 in range(16):
                                    kcc = hf * 16 + k16
                                    wsl = wbufA[:, (hf % 2) * 2048 + k16 * 128:(hf % 2) * 2048 + (k16 + 1) * 128]
                                    st = (kcc == 0)
                                    sp = (kcc == NKC - 1)
                                    t.matmul(pn64[:, 0:64], wsl,
                                             xah[:, kcc * HALO:(kcc + 1) * HALO],
                                             start=st, stop=sp)
                                    t.matmul(pga[0][:, :], wsl, xam[:, xm(kcc, 0):xm(kcc, 512)],
                                             start=st, stop=sp)
                                    t.matmul(pga[1][:, :], wsl, xam[:, xm(kcc, 512):xm(kcc, 1024)],
                                             start=st, stop=sp)
                                if hf == 0:
                                    wrel[2 * i] = C.inc("pe", t.last)
                            wrel[2 * i + 1] = C.cnt["pe"] + 1
                        gdep = []
                        if i > 0:
                            gdep = [("act", tA["eqkb"][i - 1] if i - 1 < 8 else tA["vev"][i - 1]),
                                    ("dve", tA["kfev"][i - 1])]
                        t_gemm = step("tensor", gemm, deps=gdep)
                        if 2 * i + 2 < 32:
                            block.gpsimd(lambda g, i=i: (wdma(g, 2 * i + 2), wdma(g, 2 * i + 3)))

                        if isK:
                            def eqkb(sc):
                                sc.activation(eqf[:, 0:64], pn64[:, 0:64], AF.Exp)
                                sc.activation(eqf[:, 64:576], pga[0][:, :], AF.Exp)
                                sc.activation(eqf[:, 576:1088], pga[1][:, :], AF.Exp)
                                sc.copy(kb[:, 0:64], pn64[:, 0:64])
                                sc.copy(kb[:, 64:576], pga[0][:, :])
                                sc.copy(kb[:, 576:1088], pga[1][:, :])
                            tA["eqkb"][i] = step("scalar", eqkb, deps=[
                                ("pe", t_gemm), ("dve", tA["rtlast"][i - 1] if i else 0)])

                            def kfev(v):
                                v.tensor_copy(kf[:, 0:512], pga[0][:, :])
                                v.tensor_copy(kf[:, 512:1024], pga[1][:, :])
                            tA["kfev"][i] = step("vector", kfev, deps=[
                                ("pe", t_gemm), ("act", tA["abslast"][i - 1] if i else 0)])

                            if ASTG < 2:
                                continue
                            t_sums = step("tensor", lambda t: (
                                t.matmul(pmisc[0:1, 0:512], onesbf[:, 0:1], eqf[:, 64:576],
                                         start=True, stop=True),
                                t.matmul(pmisc[32:33, 0:512], onesbf[:, 0:1], eqf[:, 576:1088],
                                         start=True, stop=True)),
                                deps=[("act", tA["eqkb"][i]),
                                      ("dve", tA["rtlast"][i - 1] if i else 0)])
                            tA["ln"][i] = step("scalar", lambda sc: (
                                sc.activation(rowsc[0:1, 0:512], pmisc[0:1, 0:512], AF.Ln),
                                sc.activation(rowsc[0:1, 512:1024], pmisc[32:33, 0:512], AF.Ln)),
                                deps=[("pe", t_sums)])
                            t_expneg = step("scalar", lambda sc: (
                                sc.activation(rowsc[0:1, 1024:1536], rowsc[0:1, 0:512], AF.Exp, scale=-1.0),
                                sc.activation(rowsc[0:1, 1536:2048], rowsc[0:1, 512:1024], AF.Exp, scale=-1.0)),
                                deps=[])

                            if ASTG < 3:
                                continue
                            for half in range(2):
                                t_abs = step("scalar", lambda sc, half=half: sc.activation(
                                    gsab[:, :], kf[:, half * 512:(half + 1) * 512], AF.Abs),
                                    deps=[("dve", tA["kfev"][i]),
                                          ("dve", tA["gssub"])])
                                step("scalar", lambda sc: sc.activation(
                                    gsab[:, :], gsab[:, :], AF.Exp, scale=-1.0), deps=[])
                                t_ln1p = step("scalar", lambda sc: sc.activation(
                                    gsab[:, :], gsab[:, :], AF.Ln, bias=1.0), deps=[])
                                tA["abslast"][i] = t_ln1p
                                step("vector", lambda v, half=half: v.tensor_scalar(
                                    gsmn[:, :], kf[:, half * 512:(half + 1) * 512], 0.0, None, ALU.min),
                                    deps=[])
                                tA["gssub"] = step("vector", lambda v: v.tensor_tensor(
                                    gsmn[:, :], gsmn[:, :], gsab[:, :], ALU.subtract),
                                    deps=[("act", t_ln1p)])
                                t_rbc = step("tensor", lambda t, half=half: t.matmul(
                                    pm2[:, 0:512], onesrf[0:1, :],
                                    rowsc[0:1, 1024 + half * 512:1536 + half * 512],
                                    start=True, stop=True),
                                    deps=[("act", t_expneg), ("dve", tA["chunk1"])])

                                for cc4 in range(4):
                                    c = half * 4 + cc4

                                    def chunk1(v, c=c, cc4=cc4):
                                        v.tensor_tensor(ksc[:, :],
                                                        eqf[:, 64 + c * 128:64 + (c + 1) * 128],
                                                        pm2[:, cc4 * 128:(cc4 + 1) * 128], ALU.mult)
                                        v.tensor_tensor_scan(cumv[:, :],
                                                             gsmn[:, cc4 * 128:(cc4 + 1) * 128],
                                                             gsmn[:, cc4 * 128:(cc4 + 1) * 128], 0.0,
                                                             ALU.add, ALU.bypass)
                                    tA["chunk1"] = step("vector", chunk1,
                                                        deps=[("pe", t_rbc), ("act", tA["eqkb"][i]),
                                                              ("act", tA["chunk2"])])

                                    def chunk1b(v, h=h):
                                        v.tensor_scalar(colsc[:, 0:1], cumv[:, 64:65], -GINV, None, ALU.mult)
                                        v.tensor_scalar(colsc[:, 1:2], cumv[:, 64:65], GINV, None, ALU.mult)
                                        v.tensor_scalar(colsc[:, 2:3], cumv[:, 127:128], GINV, None, ALU.mult)
                                        v.tensor_tensor(colsc[:, 3:4], cumv[:, 127:128],
                                                        cumv[:, 64:65], ALU.subtract)
                                        v.tensor_tensor(bsum[:, h:h + 1], bsum[:, h:h + 1],
                                                        cumv[:, 127:128], ALU.add)
                                    t_c1b = step("vector", chunk1b, deps=[("act", tA["chunk2"])])

                                    def chunk2(sc, h=h, c=c):
                                        sc.activation(epp[:, h * TPC + c * 128:h * TPC + (c + 1) * 128],
                                                      cumv[:, :], AF.Exp, scale=GINV, bias=colsc[:, 0:1])
                                        sc.activation(ene[:, :], cumv[:, :], AF.Exp, scale=-GINV,
                                                      bias=colsc[:, 1:2])
                                        sc.activation(ebc[:, h * NCH + c:h * NCH + c + 1],
                                                      cumv[:, 127:128], AF.Exp, scale=GINV)
                                        sc.activation(ecm[:, h * NCH + c:h * NCH + c + 1],
                                                      cumv[:, 64:65], AF.Exp, scale=GINV)
                                        sc.activation(fhat[:, h * NCH + c:h * NCH + c + 1],
                                                      colsc[:, 3:4], AF.Exp, scale=GINV)
                                    tA["chunk2"] = step("scalar", chunk2,
                                                        deps=[("dve", t_c1b),
                                                              ("dve", tA["ktl"][h])])

                                    tA["ktl"][h] = step("vector", lambda v, h=h, c=c: v.tensor_tensor(
                                        ktl[:, h * XT + 64 + c * 128:h * XT + 64 + (c + 1) * 128],
                                        ksc[:, :], ene[:, :], ALU.mult),
                                        deps=[("act", tA["chunk2"])])
                            tA["ktllast"][h] = tA["ktl"][h]

                            if ASTG < 4:
                                continue
                            t_rt = tA["rtlast"][i - 1] if i else 0
                            for (c0, cn) in ((0, 64), (64, 512), (576, 512)):
                                t_rk = step("tensor", lambda t, c0=c0, cn=cn: t.matmul(
                                    pmisc[:, 0:cn], renc[:, :], kb[:, c0:c0 + cn],
                                    start=True, stop=True),
                                    deps=[("act", tA["ln"][i]), ("dve", t_rt)])
                                t_rt = step("vector", lambda v, c0=c0, cn=cn: (
                                    v.tensor_tensor(rt[:, 0:cn], pmisc[:, 0:cn],
                                                    sintab[:, c0:c0 + cn], ALU.mult),
                                    v.tensor_tensor(ru[:, 0:cn], kb[:, c0:c0 + cn],
                                                    costab[:, c0:c0 + cn], ALU.mult)),
                                    deps=[("pe", t_rk)])
                                step("vector", lambda v, h=h, c0=c0, cn=cn: v.tensor_tensor(
                                    sk[:, h * XT + c0:h * XT + c0 + cn],
                                    ru[:, 0:cn], rt[:, 0:cn], ALU.add), deps=[])
                            tA["rtlast"][i] = t_rt
                        else:
                            def vev(sc):
                                sc.copy(vb[:, 0:64], pn64[:, 0:64])
                                sc.copy(vb[:, 64:576], pga[0][:, :])
                                sc.copy(vb[:, 576:1088], pga[1][:, :])
                            tA["vev"][i] = step("scalar", vev, deps=[("pe", t_gemm)])
                            tA["kfev"][i] = C.cnt["dve"]

                            cpb = [tA["ptbacopy"][i - 1]] * 2
                            for j_t in range(NXT + NCH):
                                bank = [ptbA, ptbA2][j_t % 2]
                                if j_t < NXT:
                                    tt = j_t
                                    cols = 128 if tt < 8 else 64
                                    src_ap = None
                                    t_tr = step("tensor", lambda t, tt=tt, cols=cols, bank=bank: t.transpose(
                                        bank[0:cols, 0:128],
                                        vb[:, tt * 128:tt * 128 + cols], ident[:, :]),
                                        deps=[("act", tA["vev"][i]), ("dve", cpb[j_t % 2])])
                                    cpb[j_t % 2] = step(
                                        "vector", lambda v, h=h, tt=tt, cols=cols, bank=bank: v.tensor_copy(
                                            vtk[0:cols, vt(h, tt):vt(h, tt) + 128],
                                            bank[0:cols, 0:128]),
                                        deps=[("pe", t_tr)])
                                else:
                                    cc = j_t - NXT
                                    t_tr = step("tensor", lambda t, cc=cc, bank=bank: t.transpose(
                                        bank[:, 0:128],
                                        vb[:, 64 + cc * 128:64 + (cc + 1) * 128], ident[:, :]),
                                        deps=[("dve", cpb[j_t % 2])])
                                    cpb[j_t % 2] = step(
                                        "vector", lambda v, h=h, cc=cc, bank=bank: v.tensor_copy(
                                            vtk2[:, (h * NCH + cc) * 128:(h * NCH + cc + 1) * 128],
                                            bank[:, 0:128]),
                                        deps=[("pe", t_tr)])
                            tA["ptbacopy"][i] = max(cpb)
                            tA["vtk2last"][h] = max(cpb)

                    for h in range(KVH if a_limit >= 17 else 0):
                        for c in range(NCH):
                            step("tensor", lambda t, h=h, c=c: t.transpose(
                                ptbA[:, 0:128],
                                ktl[:, h * XT + 64 + c * 128:h * XT + 64 + (c + 1) * 128],
                                ident[:, :]))
                            step("vector", lambda v: v.tensor_copy(ktok[:, :], ptbA[:, 0:128]))
                            step("tensor", lambda t, h=h, c=c: t.matmul(
                                pmisc[:, 0:128], ktok[:, :],
                                vtk2[:, (h * NCH + c) * 128:(h * NCH + c + 1) * 128],
                                start=True, stop=True))
                            step("vector", lambda v, h=h, c=c: v.tensor_tensor(
                                sct[:, :], pmisc[:, 0:128],
                                fhat[:, h * NCH + c:h * NCH + c + 1].broadcast_to((128, 128)),
                                ALU.mult))
                            step("vector", lambda v, h=h, c=c: v.tensor_tensor(
                                tseg[:, h * 128:(h + 1) * 128],
                                tseg[:, h * 128:(h + 1) * 128],
                                ebc[:, h * NCH + c:h * NCH + c + 1].broadcast_to((128, 128)),
                                ALU.mult))
                            step("vector", lambda v, h=h: v.tensor_tensor(
                                tseg[:, h * 128:(h + 1) * 128],
                                tseg[:, h * 128:(h + 1) * 128],
                                sct[:, :], ALU.add))

                    if a_limit >= 18:
                        step("scalar", lambda sc: sc.activation(ebsg[:, :], bsum[:, :], AF.Exp, scale=GINV))
                        step("tensor", lambda t: t.transpose(
                            pmisc[0:KVH, 0:128], ebsg[:, :], identf[:, :]))
                        step("vector", lambda v: v.tensor_copy(ebsgr[:, :], pmisc[0:KVH, 0:128]))

                    def ccdma(g):
                        if a_limit < 18:
                            return
                        for nm in CSEMS:
                            C.wait(g, nm)
                        for h in range(KVH):
                            st_c.issue(g, lambda g=g, h=h: g.dma_start(
                                cc_in[h * 129:h * 129 + 128, :], tseg[:, h * 128:(h + 1) * 128]))
                            st_c.issue(g, lambda g=g, h=h: g.dma_start(
                                cc_in[h * 129 + 128:h * 129 + 129, :], ebsgr[h:h + 1, :]))
                    block.gpsimd(ccdma)

                    tap("sk0", sk[:, 0:XT])
                    tap("vtk0", vtk[:, 0:NXT * 128])
                    tap("ktl0", ktl[:, 0:XT])
                    tap("vtk2_0", vtk2[:, 0:NCH * 128])
                    tap("fhat", fhat[:, :])
                    tap("epp0", epp[:, 0:TPC])
                    tap("tseg", tseg[:, :])
                    tap("ebsg", ebsg[:, :])
                    tap("ebc", ebc[:, :])
                    tap("ecm", ecm[:, :])
                    end_block(block)

        es_halo.close()

        if "C" in phases:
            with contextlib.ExitStack() as esC:
                sbC, psC = mk(esC)
                ccsb = sbC("ccsb", [128, 4 * 129], F32)
                brsb = sbC("brsb", [1, 4 * 128], F32)
                pmC = psC("pmC", [128, 4], F32)

                with nc.Block() as block:
                    cur_block[0] = block

                    def coll(g):
                        if c_limit < 0:
                            return
                        C.inc("cc", g.collective_compute(
                            "AllGather", ALU.bypass,
                            replica_groups=[list(range(NCORE))],
                            ins=[cc_in.ap().opt()],
                            outs=[cc_out.ap().opt()]))
                    block.gpsimd(coll)

                    step("vector", lambda v: v.memset(tseg[:, :], 0))
                    ldj_toks = []
                    comp_dve = []
                    btr_pe = []

                    NB = 4
                    for h in range(KVH):
                        for j in range(NCORE):
                            it = h * NCORE + j
                            b = it % NB

                            def ldj(g, h=h, j=j, it=it, b=b):
                                if it == 0:
                                    C.wait(g, "cc")
                                    C.wait(g, "dve")
                                if it >= NB:
                                    C.wait(g, "dve", comp_dve[it - NB])
                                base = (j * KVH + h) * 129
                                ldj_toks.append(st_g.issue(g, lambda g=g, base=base, b=b: g.dma_start(
                                    ccsb[:, b * 129:b * 129 + 128], cc_out[base:base + 128, :])))
                                ldj_toks.append(st_g.issue(g, lambda g=g, base=base, b=b: g.dma_start(
                                    brsb[0:1, b * 128:(b + 1) * 128], cc_out[base + 128:base + 129, :])))
                            block.gpsimd(ldj)
                            dv = list(ldj_toks[-2:])

                            def btr(t, b=b, dv=dv, it=it):
                                st_g.wait(t, dv[1])
                                if it >= NB:
                                    t.wait_ge(C.sems["dve"], comp_dve[it - NB])
                                t.transpose(pmC[0:128, b:b + 1], brsb[0:1, b * 128:(b + 1) * 128], identf[0:1, 0:1])
                            btr_pe.append(step("tensor", btr, deps=[]))

                            def comp1(v, h=h, j=j, b=b, dv=dv):
                                st_g.wait(v, dv[0])
                                v.tensor_tensor(colsc[:, 0:1], pmC[0:128, b:b + 1],
                                                cmask[:, j:j + 1], ALU.mult)
                            step("vector", comp1, deps=[("pe", btr_pe[-1])])

                            def comp2(v, h=h, j=j, b=b):
                                v.tensor_tensor(colsc[:, 0:1], colsc[:, 0:1],
                                                cmask[:, 8 + j:9 + j], ALU.add)
                                v.tensor_tensor(ccsb[:, b * 129:b * 129 + 128],
                                                ccsb[:, b * 129:b * 129 + 128],
                                                cmask[:, j:j + 1].broadcast_to((128, 128)), ALU.mult)
                            step("vector", comp2, deps=[])

                            step("vector", lambda v, h=h: v.tensor_tensor(
                                tseg[:, h * 128:(h + 1) * 128],
                                tseg[:, h * 128:(h + 1) * 128],
                                colsc[:, 0:1].broadcast_to((128, 128)), ALU.mult), deps=[])
                            comp_dve.append(step("vector", lambda v, h=h, b=b: v.tensor_tensor(
                                tseg[:, h * 128:(h + 1) * 128],
                                tseg[:, h * 128:(h + 1) * 128],
                                ccsb[:, b * 129:b * 129 + 128], ALU.add), deps=[]))

                    tap("sin", tseg[:, :])
                    end_block(block)

        if "B" in phases:
            with contextlib.ExitStack() as esB:
                sbB, psB = mk(esB)
                wbufB = sbB("wbufB", [128, 2 * 8 * 128], BF16)
                qb = sbB("qb", [128, TPC], BF16)
                eqq = sbB("eqq", [128, TPC], BF16)
                qsb = sbB("qsb", [128, 4 * TPC], BF16)
                sqb = sbB("sqb", [128, 4 * TPC], BF16)
                ru = sbB("ruB", [128, 512], F32)
                rt = sbB("rtB", [128, 512], BF16)
                stl = sbB("stl", [128, NCH * 128], BF16)
                ktokg = sbB("ktokg", [128, NCH * 128], BF16)
                scrB = sbB("scrB", [128, NCH * 128], BF16)
                sctB8 = scrB
                epg = sbB("epg", [128, 2 * 4 * WIN], BF16)
                p1t = sbB("p1t", [128, 512], BF16)
                p2t = sbB("p2t", [128, 512], BF16)
                ab1 = scrB[:, 0:512]
                ab2 = scrB[:, 512:1024]
                sums = sbB("sums", [128, 2 * 8], F32)
                omst = sbB("omst", [128, 512], BF16)
                pq = [psB(f"pq{i}", [128, 512], F32) for i in range(2)]
                pmB = psB("pmB", [128, 512], F32)
                pm3 = psB("pm3B", [128, 512], F32)
                poB = psB("poB", [128, 512], F32)
                pAT = psB("pAT", [128, 512], F32)
                ptbB = psB("ptbB", [128, 1024], BF16)

                with nc.Block() as block:
                    cur_block[0] = block
                    wqdone = []
                    wqrel = [None] * (4 * H)
                    om_dma = []

                    def wqdma(g, s):
                        qh = s // 4
                        qq = s % 4
                        if s >= 2:
                            g.wait_ge(C.sems["pe"], wqrel[s - 2])
                        wqdone.append(st_w.issue(g, lambda g=g, s=s, qq=qq, qh=qh: g.dma_start(
                            wbufB[:, bass.ts(s % 2, 8 * 128)].rearrange(
                                "p (kc n) -> p kc n", kc=8),
                            wq.rearrange("(kc p) n -> p kc n", p=128)[:, 8 * qq:8 * qq + 8,
                                                                     qh * 128:(qh + 1) * 128])))

                    block.gpsimd(lambda g: (wqdma(g, 0), wqdma(g, 1)))

                    tB = dict(eqq=0, scoreexp=[0, 0], sctlast=0, pm3read=0, qsbreadpe=0,
                              sqreadpe=0, rt=0, oev=0, ptev=0, abev=0, pnorm=[0, 0],
                              o=0, sqaddlast=0, qt2last=0, lnlast=0, ptrlast=0,
                              pmask=[0, 0], stlread=0, tmmlast=0, ktokev=0)
                    post = [None]
                    for g_i in range(KVH):
                        def ktrs(t, g_i=g_i):
                            for c in range(NCH):
                                t.transpose(ptbB[:, bass.ts(c % 8, 128)],
                                            ktl[:, g_i * XT + 64 + c * 128:g_i * XT + 64 + (c + 1) * 128],
                                            ident[:, :])
                        t_ktr = step("tensor", ktrs,
                                     deps=[("dve", max(tB["ptev"], tB["ktokev"]))])
                        t_kte = step("vector", lambda v: v.tensor_copy(ktokg[:, :], ptbB[:, :]),
                                     deps=[("pe", t_ktr)])
                        tB["ktokev"] = t_kte

                        def tmms(t, g_i=g_i):
                            for c in range(NCH):
                                pb = [poB, pAT][c // 4]
                                t.matmul(pb[:, (c % 4) * 128:(c % 4 + 1) * 128],
                                         ktokg[:, bass.ts(c, 128)],
                                         vtk2[:, (g_i * NCH + c) * 128:(g_i * NCH + c + 1) * 128],
                                         start=True, stop=True)
                        t_tmm = step("tensor", tmms,
                                     deps=[("dve", t_kte), ("dve", max(tB["oev"], tB["abev"]))])
                        tB["tmmlast"] = t_tmm
                        t_sct = step("vector", lambda v, g_i=g_i: (
                            v.tensor_tensor(sctB8[:, 0:512].rearrange("p (c n) -> p c n", c=4),
                                            poB[:, :].rearrange("p (c n) -> p c n", c=4),
                                            fhat[:, g_i * NCH:g_i * NCH + 4].unsqueeze(2).broadcast_to((128, 4, 128)),
                                            ALU.mult),
                            v.tensor_tensor(sctB8[:, 512:1024].rearrange("p (c n) -> p c n", c=4),
                                            pAT[:, :].rearrange("p (c n) -> p c n", c=4),
                                            fhat[:, g_i * NCH + 4:g_i * NCH + 8].unsqueeze(2).broadcast_to((128, 4, 128)),
                                            ALU.mult)),
                            deps=[("pe", t_tmm)])
                        for c in range(NCH):
                            step("vector", lambda v, g_i=g_i, c=c: v.scalar_tensor_tensor(
                                stl[:, c * 128:(c + 1) * 128],
                                tseg[:, g_i * 128:(g_i + 1) * 128], 0.5,
                                ecm[:, g_i * NCH + c:g_i * NCH + c + 1].broadcast_to((128, 128)),
                                ALU.mult, ALU.mult),
                                deps=[("pe", tB["stlread"])])
                            step("vector", lambda v, g_i=g_i, c=c: v.tensor_tensor(
                                tseg[:, g_i * 128:(g_i + 1) * 128],
                                tseg[:, g_i * 128:(g_i + 1) * 128],
                                ebc[:, g_i * NCH + c:g_i * NCH + c + 1].broadcast_to((128, 128)),
                                ALU.mult), deps=[])
                            tB["sctlast"] = step("vector", lambda v, g_i=g_i, c=c: v.tensor_tensor(
                                tseg[:, g_i * 128:(g_i + 1) * 128],
                                tseg[:, g_i * 128:(g_i + 1) * 128],
                                sctB8[:, c * 128:(c + 1) * 128], ALU.add), deps=[])

                        def make_post(qh, hh):
                            def post_fn():
                                t_sums = step("tensor", lambda t: (
                                    t.matmul(pmB[0:1, 0:512], onesb[:, 0:1], eqq[:, 0:512],
                                             start=True, stop=True),
                                    t.matmul(pmB[32:33, 0:512], onesb[:, 0:1], eqq[:, 512:1024],
                                             start=True, stop=True)),
                                    deps=[("act", tB["eqq"]), ("dve", tB["sctlast"])])
                                t_ln = step("scalar", lambda sc: (
                                    sc.activation(rowsc[0:1, 0:512], pmB[0:1, 0:512], AF.Ln),
                                    sc.activation(rowsc[0:1, 512:1024], pmB[32:33, 0:512], AF.Ln)),
                                    deps=[("pe", t_sums)])
                                tB["lnlast"] = t_ln
                                t_expneg = step("scalar", lambda sc: (
                                    sc.activation(rowsc[0:1, 1024:1536], rowsc[0:1, 0:512], AF.Exp, scale=-1.0),
                                    sc.activation(rowsc[0:1, 1536:2048], rowsc[0:1, 512:1024], AF.Exp, scale=-1.0)),
                                    deps=[])
                                for half in range(2):
                                    t_rbc = step("tensor", lambda t, half=half: t.matmul(
                                        pm3[:, 0:512], onesrf[0:1, :],
                                        rowsc[0:1, 1024 + half * 512:1536 + half * 512],
                                        start=True, stop=True),
                                        deps=[("act", t_expneg), ("dve", tB["pm3read"])])
                                    t_qt1 = step("vector", lambda v, half=half: v.tensor_tensor(
                                        ru[:, :],
                                        eqq[:, half * 512:(half + 1) * 512],
                                        pm3[:, 0:512], ALU.mult),
                                        deps=[("pe", t_rbc)])
                                    tB["pm3read"] = t_qt1
                                    tB["qt2last"] = step("vector", lambda v, g_i=g_i, hh=hh, half=half: v.tensor_tensor(
                                        qsb[:, hh * TPC + half * 512:hh * TPC + (half + 1) * 512],
                                        ru[:, :],
                                        epp[:, g_i * TPC + half * 512:g_i * TPC + (half + 1) * 512],
                                        ALU.mult),
                                        deps=[("pe", tB["qsbreadpe"])])
                                    t_rq = step("tensor", lambda t, half=half: t.matmul(
                                        pm3[:, 0:512], renc[:, :], qb[:, half * 512:(half + 1) * 512],
                                        start=True, stop=True),
                                        deps=[("act", tB["eqq"]), ("dve", t_qt1)])
                                    tB["rt"] = step("vector", lambda v, half=half: (
                                        v.tensor_tensor(rt[:, :], pm3[:, 0:512],
                                                        sintab[:, 64 + half * 512:64 + (half + 1) * 512], ALU.mult),
                                        v.tensor_tensor(ru[:, :], qb[:, half * 512:(half + 1) * 512],
                                                        costab[:, 64 + half * 512:64 + (half + 1) * 512], ALU.mult)),
                                        deps=[("pe", t_rq)])
                                    tB["pm3read"] = tB["rt"]
                                    tB["sqaddlast"] = step("vector", lambda v, hh=hh, half=half: v.tensor_tensor(
                                        sqb[:, hh * TPC + half * 512:hh * TPC + (half + 1) * 512],
                                        ru[:, :], rt[:, :], ALU.add),
                                        deps=[("pe", tB["sqreadpe"])])
                            return post_fn

                        for hh in range(4):
                            qh = 4 * g_i + hh

                            for gh in range(2):
                                def qgemm(t, qh=qh, gh=gh):
                                    for qq in (2 * gh, 2 * gh + 1):
                                        st_w.wait(t, wqdone[4 * qh + qq])
                                        for k8 in range(8):
                                            kcc = qq * 8 + k8
                                            wsl = wbufB[:, (qq % 2) * 1024 + k8 * 128:(qq % 2) * 1024 + (k8 + 1) * 128]
                                            for n in range(2):
                                                t.matmul(pq[n][:, :], wsl,
                                                         xam[:, xm(kcc, n * 512):xm(kcc, (n + 1) * 512)],
                                                         start=(kcc == 0), stop=(kcc == NKC - 1))
                                        if qq == 2 * gh:
                                            wqrel[4 * qh + qq] = C.inc("pe", t.last)
                                        else:
                                            wqrel[4 * qh + qq] = C.cnt["pe"] + 1
                                t_qg = step("tensor", qgemm,
                                            deps=[("act", max(tB["eqq"], max(tB["scoreexp"])))])
                                for s_n in range(4 * qh + 2 * gh + 2, min(4 * qh + 2 * gh + 4, 4 * H)):
                                    block.gpsimd(lambda g, s_n=s_n: wqdma(g, s_n))

                            if post[0] is not None:
                                post[0]()

                            tB["eqq"] = step("scalar", lambda sc: (
                                sc.activation(eqq[:, 0:512], pq[0][:, :], AF.Exp),
                                sc.activation(eqq[:, 512:1024], pq[1][:, :], AF.Exp),
                                sc.copy(qb[:, 0:512], pq[0][:, :]),
                                sc.copy(qb[:, 512:1024], pq[1][:, :])),
                                deps=[("pe", t_qg), ("dve", tB["rt"])])

                            post[0] = make_post(qh, hh)
                        post[0]()
                        post[0] = None

                        def stageA(c, g_i=g_i):
                            p = c % 2
                            eo = p * 4 * WIN

                            def scmm(t, g_i=g_i, c=c):
                                for hh in range(4):
                                    t.matmul(pq[hh // 2][:, (hh % 2) * WIN:(hh % 2 + 1) * WIN],
                                             sqb[:, hh * TPC + c * 128:hh * TPC + (c + 1) * 128],
                                             sk[:, g_i * XT + c * 128:g_i * XT + c * 128 + WIN],
                                             start=True, stop=True)
                            t_sc = step("tensor", scmm,
                                        deps=[("dve", tB["sqaddlast"]),
                                              ("act", tB["scoreexp"][(c + 1) % 2])])
                            tB["sqreadpe"] = t_sc
                            tB["scoreexp"][p] = step("scalar", lambda sc, eo=eo: (
                                sc.activation(epg[:, eo:eo + 2 * WIN], pq[0][:, 0:2 * WIN], AF.Exp, scale=ISQD),
                                sc.activation(epg[:, eo + 2 * WIN:eo + 4 * WIN], pq[1][:, 0:2 * WIN], AF.Exp, scale=ISQD)),
                                deps=[("pe", t_sc), ("dve", tB["pnorm"][p]),
                                      ("pe", tB["ptrlast"])])

                            moff = WIN if c == 0 else 0

                            def pmask(v, moff=moff, eo=eo, p=p):
                                for hh in range(4):
                                    v.scalar_tensor_tensor(
                                        epg[:, eo + hh * WIN:eo + (hh + 1) * WIN],
                                        epg[:, eo + hh * WIN:eo + (hh + 1) * WIN], 1.0,
                                        maskswa[:, moff:moff + WIN],
                                        ALU.mult, ALU.mult,
                                        accum_out=sums[:, p * 8 + hh:p * 8 + hh + 1])
                            tB["pmask"][p] = step("vector", pmask, deps=[("act", tB["scoreexp"][p])])
                            t_recip = step("vector", lambda v, p=p: v.reciprocal(
                                sums[:, p * 8 + 4:p * 8 + 8], sums[:, p * 8:p * 8 + 4]), deps=[])
                            t_bounce = step("scalar", lambda sc: None, deps=[("dve", t_recip)])

                            def pnorm(v, eo=eo, p=p):
                                for hh in range(4):
                                    v.tensor_scalar(epg[:, eo + hh * WIN:eo + (hh + 1) * WIN],
                                                    epg[:, eo + hh * WIN:eo + (hh + 1) * WIN],
                                                    sums[:, p * 8 + 4 + hh:p * 8 + 5 + hh], 0.5,
                                                    ALU.mult, ALU.mult)
                            tB["pnorm"][p] = step("vector", pnorm, deps=[("act", t_bounce)])

                        def stageB(c, g_i=g_i):
                            p = c % 2
                            eo = p * 4 * WIN

                            def ptr(t, eo=eo):
                                for hh in range(4):
                                    t.transpose(ptbB[:, hh * 128:(hh + 1) * 128],
                                                epg[:, eo + hh * WIN:eo + hh * WIN + 128], ident[:, :])
                                    t.transpose(ptbB[0:64, 512 + hh * 128:512 + (hh + 1) * 128],
                                                epg[:, eo + hh * WIN + 128:eo + hh * WIN + WIN], ident[:, :])
                            t_ptr = step("tensor", ptr,
                                         deps=[("dve", tB["pnorm"][p]),
                                               ("dve", max(tB["ptev"], tB["ktokev"]))])
                            tB["ptrlast"] = t_ptr
                            tB["ptev"] = step("vector", lambda v: (
                                v.tensor_copy(p1t[:, :], ptbB[:, 0:512]),
                                v.tensor_copy(p2t[0:64, :], ptbB[0:64, 512:1024])),
                                deps=[("pe", t_ptr), ("pe", tB["o"])])

                            t_at = step("tensor", lambda t, g_i=g_i, c=c: (
                                t.matmul(pAT[:, :],
                                         ktl[:, g_i * XT + c * 128:g_i * XT + c * 128 + 128],
                                         qsb[:, :].rearrange("p (h n) -> p h n", h=4)[:, :, c * 128:(c + 1) * 128],
                                         start=True, stop=True),
                                t.matmul(pm3[0:64, 0:512],
                                         ktl[:, g_i * XT + c * 128 + 128:g_i * XT + c * 128 + WIN],
                                         qsb[:, :].rearrange("p (h n) -> p h n", h=4)[:, :, c * 128:(c + 1) * 128],
                                         start=True, stop=True)),
                                deps=[("dve", tB["qt2last"]),
                                      ("dve", max(tB["abev"], tB["pm3read"], tB["sctlast"]))])
                            tB["qsbreadpe"] = t_at

                            tB["abev"] = step("vector", lambda v: (
                                v.tensor_tensor(ab1[:, :].rearrange("p (h n) -> p h n", h=4),
                                                pAT[:, :].rearrange("p (h n) -> p h n", h=4),
                                                trimask[:, 0:128].unsqueeze(1).broadcast_to((128, 4, 128)),
                                                ALU.mult),
                                v.tensor_tensor(ab2[0:64, :].rearrange("p (h n) -> p h n", h=4),
                                                pm3[0:64, 0:512].rearrange("p (h n) -> p h n", h=4),
                                                trimask[0:64, 128:256].unsqueeze(1).broadcast_to((64, 4, 128)),
                                                ALU.mult)),
                                deps=[("pe", t_at)])
                            tB["pm3read"] = tB["abev"]

                            tB["o"] = step("tensor", lambda t, g_i=g_i, c=c: (
                                t.matmul(poB[:, :], vtk[:, vt(g_i, c):vt(g_i, c) + 128],
                                         p1t[:, :], start=True, stop=False),
                                t.matmul(poB[:, :], vtk[0:64, vt(g_i, c + 1):vt(g_i, c + 1) + 128],
                                         p2t[0:64, :], start=False, stop=False),
                                t.matmul(poB[:, :], stl[:, c * 128:(c + 1) * 128],
                                         qsb[:, :].rearrange("p (h n) -> p h n", h=4)[:, :, c * 128:(c + 1) * 128],
                                         start=False, stop=False),
                                t.matmul(poB[:, :], vtk[:, vt(g_i, c):vt(g_i, c) + 128],
                                         ab1[:, :], start=False, stop=False),
                                t.matmul(poB[:, :], vtk[0:64, vt(g_i, c + 1):vt(g_i, c + 1) + 128],
                                         ab2[0:64, :], start=False, stop=True)),
                                deps=[("dve", tB["ptev"]), ("dve", tB["abev"]),
                                      ("dve", tB["oev"])])
                            tB["qsbreadpe"] = tB["o"]
                            tB["stlread"] = tB["o"]

                            def oev(v, g_i=g_i, c=c):
                                if om_dma:
                                    st_c.wait(v, om_dma[-1])
                                v.tensor_copy(omst[:, :], poB[:, :])
                            tB["oev"] = step("vector", oev, deps=[("pe", tB["o"])])

                            def omdma(g, g_i=g_i, c=c):
                                C.wait(g, "dve", tB["oev"])
                                om_dma.append(st_c.issue(g, lambda g=g, g_i=g_i, c=c: g.dma_start(
                                    omix_d[:, :].rearrange("p (h n) -> p h n", h=H)[:, 4 * g_i:4 * g_i + 4, c * 128:(c + 1) * 128],
                                    omst[:, :].rearrange("p (h n) -> p h n", h=4))))
                            block.gpsimd(omdma)

                        stageA(0)
                        for c in range(NCH):
                            if c + 1 < NCH:
                                stageA(c + 1)
                            stageB(c)

                    tap("omix0", omix_d[:, 0:TPC])
                    tap("omix4", omix_d[:, 4 * TPC:5 * TPC])
                    tap("omixall", omix_d[:, 0:H * TPC])
                    end_block(block)

        if "W" in phases:
            ret.close()
            with contextlib.ExitStack() as esW:
                sbW, psW = mk(esW)
                wobuf = sbW("wobuf", [128, 2 * 8 * 512], BF16)
                ombuf = sbW("ombuf", [128, 2 * 8 * TPC], BF16)
                sqsc = sbW("sqsc", [128, 512], BF16)
                rstb = sbW("rstb", [128, TPC], BF16)
                mrb = sbW("mrb", [128, TPC], BF16)
                rowsc2 = sbW("rowsc2", [1, 2048], F32)
                rowb = sbW("rowb", [1, 2048], BF16)
                pw = [psW(f"pw{i}", [128, 512], F32) for i in range(8)]

                with nc.Block() as block:
                    cur_block[0] = block
                    wodma = []
                    omdma2 = []
                    womm_pe = []
                    yev_dve = []

                    KG = 8
                    for mg in range(8):
                        for kg in range(NKC // KG):
                            i = mg * (NKC // KG) + kg

                            def wdma(g, i=i, mg=mg, kg=kg):
                                if i >= 2:
                                    g.wait_ge(C.sems["pe"], womm_pe[(i - 1) * KG - 1])
                                g.dma_start(
                                    wobuf[:, (i % 2) * KG * 512:(i % 2 + 1) * KG * 512].rearrange(
                                        "p (k n) -> p k n", k=KG),
                                    wo.rearrange("(kc p) n -> p kc n", p=128)[:, kg * KG:(kg + 1) * KG,
                                                                             mg * 512:(mg + 1) * 512])
                                wodma.append(C.inc("wdma0", g.last, 16) if False else None)
                            def wdma2(g, i=i, mg=mg, kg=kg):
                                if i >= 2:
                                    g.wait_ge(C.sems["pe"], womm_pe[(i - 1) * KG - 1])
                                wodma.append(st_w.issue(g, lambda g=g, mg=mg, kg=kg: g.dma_start(
                                    wobuf[:, (i % 2) * KG * 512:(i % 2 + 1) * KG * 512].rearrange(
                                        "p (k n) -> p k n", k=KG),
                                    wo.rearrange("(kc p) n -> p kc n", p=128)[:, kg * KG:(kg + 1) * KG,
                                                                             mg * 512:(mg + 1) * 512])))
                            block.gpsimd(wdma2)

                            def odma(s, i=i, kg=kg):
                                if i >= 2:
                                    s.wait_ge(C.sems["pe"], womm_pe[(i - 1) * KG - 1])
                                omdma2.append(st_a.issue(s, lambda s=s, kg=kg: s.dma_start(
                                    ombuf[:, (i % 2) * KG * TPC:(i % 2 + 1) * KG * TPC].rearrange(
                                        "p (k n) -> p k n", k=KG),
                                    omix_d[:, kg * KG * TPC:(kg + 1) * KG * TPC].rearrange(
                                        "p (k n) -> p k n", k=KG))))
                            block.sync(odma)

                            def womm(t, i=i, mg=mg, kg=kg):
                                st_w.wait(t, wodma[i])
                                st_a.wait(t, omdma2[i])
                                if kg == 0:
                                    if mg == 0:
                                        for nm in ("dve", "act", "pool"):
                                            if C.cnt[nm]:
                                                C.wait(t, nm)
                                    else:
                                        t.wait_ge(C.sems["dve"], yev_dve[mg - 1])
                                for k4 in range(KG):
                                    kcc = kg * KG + k4
                                    mm = None
                                    for m in range(4):
                                        for n in range(2):
                                            mm = t.matmul(
                                                pw[m * 2 + n][:, :],
                                                wobuf[:, (i % 2) * KG * 512 + k4 * 512 + m * 128:(i % 2) * KG * 512 + k4 * 512 + (m + 1) * 128],
                                                ombuf[:, (i % 2) * KG * TPC + k4 * TPC + n * 512:(i % 2) * KG * TPC + k4 * TPC + (n + 1) * 512],
                                                start=(kcc == 0), stop=(kcc == NKC - 1))
                                    womm_pe.append(C.inc("pe", mm))
                            block.tensor(womm)

                        def yev(v, mg=mg):
                            v.wait_ge(C.sems["pe"], womm_pe[-1])
                            for m in range(4):
                                kcg = mg * 4 + m
                                for n in range(2):
                                    v.tensor_tensor(
                                        xam[:, xm(kcg, n * 512):xm(kcg, (n + 1) * 512)],
                                        pw[m * 2 + n][:, :],
                                        xam[:, xm(kcg, n * 512):xm(kcg, (n + 1) * 512)],
                                        ALU.add)
                            yev_dve.append(C.inc("dve", v.engine_nop()))
                        block.vector(yev)

                    for kcg in range(NKC):
                        step("scalar", lambda sc, kcg=kcg: sc.activation(
                            sqsc[:, :], xam[:, xm(kcg, 0):xm(kcg, 512)], AF.Square))
                        step("tensor", lambda t, kcg=kcg: (
                            t.matmul(pw[4][0:1, :], onesb[:, 0:1], xam[:, xm(kcg, 0):xm(kcg, 512)],
                                     start=(kcg == 0), stop=(kcg == NKC - 1)),
                            t.matmul(pw[5][0:1, :], onesb[:, 0:1], xam[:, xm(kcg, 512):xm(kcg, 1024)],
                                     start=(kcg == 0), stop=(kcg == NKC - 1)),
                            t.matmul(pw[6][0:1, :], onesb[:, 0:1], sqsc[:, :],
                                     start=(kcg == 0), stop=(kcg == NKC - 1))))
                        step("scalar", lambda sc, kcg=kcg: sc.activation(
                            sqsc[:, :], xam[:, xm(kcg, 512):xm(kcg, 1024)], AF.Square))
                        step("tensor", lambda t, kcg=kcg: t.matmul(
                            pw[7][0:1, :], onesb[:, 0:1], sqsc[:, :],
                            start=(kcg == 0), stop=(kcg == NKC - 1)))

                    def rows1(v):
                        v.tensor_scalar(rowsc[0:1, 0:512], pw[4][0:1, :], 1.0 / DM, None, ALU.mult)
                        v.tensor_scalar(rowsc[0:1, 512:1024], pw[5][0:1, :], 1.0 / DM, None, ALU.mult)
                        v.tensor_scalar(rowsc2[0:1, 0:512], pw[6][0:1, :], 1.0 / DM, None, ALU.mult)
                        v.tensor_scalar(rowsc2[0:1, 512:1024], pw[7][0:1, :], 1.0 / DM, None, ALU.mult)
                    step("vector", rows1)
                    step("vector", lambda v: v.tensor_tensor(
                        rowsc2[0:1, 1024:2048], rowsc[0:1, 0:1024], rowsc[0:1, 0:1024], ALU.mult))
                    step("vector", lambda v: v.tensor_tensor(
                        rowsc2[0:1, 0:1024], rowsc2[0:1, 0:1024], rowsc2[0:1, 1024:2048], ALU.subtract))
                    step("vector", lambda v: v.tensor_scalar(
                        rowsc2[0:1, 0:1024], rowsc2[0:1, 0:1024], EPS, None, ALU.add))
                    step("scalar", lambda sc: sc.activation(
                        rowsc[0:1, 1024:2048], rowsc2[0:1, 0:1024], AF.Ln))
                    step("scalar", lambda sc: sc.activation(
                        rowsc[0:1, 1024:2048], rowsc[0:1, 1024:2048], AF.Exp, scale=-0.5))
                    step("vector", lambda v: v.tensor_tensor(
                        rowsc2[0:1, 1024:2048], rowsc[0:1, 0:1024],
                        rowsc[0:1, 1024:2048], ALU.mult))
                    step("vector", lambda v: (
                        v.tensor_copy(rowb[0:1, 0:1024], rowsc[0:1, 1024:2048]),
                        v.tensor_copy(rowb[0:1, 1024:2048], rowsc2[0:1, 1024:2048])))

                    for half in range(2):
                        step("tensor", lambda t, half=half: t.matmul(
                            pw[0][:, :], onesr[0:1, :], rowb[0:1, half * 512:(half + 1) * 512],
                            start=True, stop=True))
                        step("vector", lambda v, half=half: v.tensor_copy(
                            rstb[:, half * 512:(half + 1) * 512], pw[0][:, :]))
                        step("tensor", lambda t, half=half: t.matmul(
                            pw[1][:, :], onesr[0:1, :], rowb[0:1, 1024 + half * 512:1536 + half * 512],
                            start=True, stop=True))
                        step("vector", lambda v, half=half: v.tensor_copy(
                            mrb[:, half * 512:(half + 1) * 512], pw[1][:, :]))

                    def norm1a(v):
                        for kcg in range(NKC):
                            for half in range(2):
                                sl = xam[:, xm(kcg, half * 512):xm(kcg, (half + 1) * 512)]
                                v.tensor_tensor(sl, sl, rstb[:, half * 512:(half + 1) * 512], ALU.mult)
                    step("vector", norm1a)

                    def norm1b(v):
                        for kcg in range(NKC):
                            for half in range(2):
                                sl = xam[:, xm(kcg, half * 512):xm(kcg, (half + 1) * 512)]
                                v.tensor_tensor(sl, sl, mrb[:, half * 512:(half + 1) * 512], ALU.subtract)
                    step("vector", norm1b)

                    tap("x1_0", xam[:, xm(0, 0):xm(0, TPC)])
                    tap("x1_1", xam[:, xm(1, 0):xm(1, TPC)])
                    end_block(block)
        else:
            ret.close()

        if "F" in phases:
            with contextlib.ExitStack() as esF:
                sbF, psF = mk(esF)
                latb = sbF("latb", [128, NKC * NER], BF16)
                lbt1 = sbF("lbt1", [128, DM], BF16)
                lbt2 = sbF("lbt2", [96, DM], BF16)
                gwtb = sbF("gwtb", [128, NKC * NEXP], BF16)
                xexpb = sbF("xexpb", [28, NER], BF16)
                lsb = sbF("lsb", [28, TPC], BF16)
                ltk = sbF("ltk", [128, NEXP], F32)
                gsc = sbF("gsc", [128, 12], F32)
                gtk = sbF("gtk", [128, NEXP], BF16)
                gfm = sbF("gfm", [28, TPC], BF16)
                grep1 = sbF("grep1", [128, TPC], BF16)
                grep2 = sbF("grep2", [96, TPC], BF16)
                htl1 = sbF("htl1", [128, TPC], BF16)
                htl2 = sbF("htl2", [96, TPC], BF16)
                bwbuf = sbF("bwbuf", [128, 2 * 4 * 384], BF16)
                y2b = sbF("y2b", [128, NKC * TPC], BF16)
                sqsc2 = sbF("sqsc2", [128, 512], BF16)
                mcol = sbF("mcol", [128, 4], F32)
                xstg = sbF("xstg", [128, 2 * 1024], F32)
                rowsc2 = sbF("rowscF", [1, 1024], F32)
                pf = [psF(f"pf{i}", [128, 512], F32) for i in range(7)]
                ptbF = psF("ptbF", [128, 1024], BF16)

                with nc.Block() as block:
                    cur_block[0] = block

                    sm_toks = []

                    def ldsm(g):
                        for dst, src in ((latb[:, :].rearrange("p (kc n) -> p kc n", kc=NKC),
                                          lat.rearrange("(kc p) n -> p kc n", p=128)),
                                         (lbt1[:, :], lbt[0:128, :]),
                                         (lbt2[:, :], lbt[128:224, :]),
                                         (gwtb[:, :].rearrange("p (kc n) -> p kc n", kc=NKC),
                                          gwt.rearrange("(kc p) n -> p kc n", p=128)),
                                         (xexpb[:, :], xexph)):
                            sm_toks.append(st_g.issue(g, lambda g=g, dst=dst, src=src:
                                                      g.dma_start(dst, src)))
                    block.gpsimd(ldsm)

                    def lg(t):
                        st_g.wait_all(t)
                        for n in range(2):
                            for kcc in range(NKC):
                                t.matmul(pf[n][0:28, :],
                                         gwtb[:, kcc * NEXP:(kcc + 1) * NEXP],
                                         xam[:, xm(kcc, n * 512):xm(kcc, (n + 1) * 512)],
                                         start=(kcc == 0), stop=(kcc == NKC - 1))
                    step("tensor", lg)
                    step("vector", lambda v: (
                        v.tensor_copy(lsb[:, 0:512], pf[0][0:28, :]),
                        v.tensor_copy(lsb[:, 512:1024], pf[1][0:28, :])))

                    for tcj in range(NCH):
                        step("tensor", lambda t, tcj=tcj: t.transpose(
                            ptbF[0:128, 0:28], lsb[:, tcj * 128:(tcj + 1) * 128], ident[0:28, 0:28]))

                        step("vector", lambda v: v.tensor_copy(ltk[:, :], ptbF[0:128, 0:28]))
                        step("vector", lambda v: v.tensor_reduce(
                            gsc[:, 0:1], ltk[:, :], mybir.AxisListType.X, ALU.max))
                        step("vector", lambda v: v.tensor_tensor(
                            gtk[:, :], ltk[:, :],
                            gsc[:, 0:1].broadcast_to((128, NEXP)), ALU.is_ge))
                        step("vector", lambda v: v.scalar_tensor_tensor(
                            ltk[:, :], gtk[:, :], -1e9, ltk[:, :], ALU.mult, ALU.add))
                        step("vector", lambda v: v.tensor_reduce(
                            gsc[:, 1:2], ltk[:, :], mybir.AxisListType.X, ALU.max))
                        step("vector", lambda v: v.tensor_tensor(
                            gsc[:, 2:3], gsc[:, 1:2], gsc[:, 0:1], ALU.subtract))
                        step("scalar", lambda sc: sc.activation(gsc[:, 3:4], gsc[:, 2:3], AF.Exp))
                        step("vector", lambda v: v.tensor_scalar(
                            gsc[:, 4:5], gsc[:, 3:4], 1.0, None, ALU.add))
                        step("vector", lambda v: v.reciprocal(gsc[:, 5:6], gsc[:, 4:5]))
                        step("vector", lambda v: (
                            v.tensor_tensor(gsc[:, 6:7], gsc[:, 5:6], gsc[:, 3:4], ALU.mult),
                            v.tensor_tensor(ltk[:, :], ltk[:, :],
                                            gsc[:, 1:2].broadcast_to((128, NEXP)), ALU.is_ge)))
                        step("vector", lambda v: (
                            v.tensor_tensor(gtk[:, :], gtk[:, :],
                                            gsc[:, 5:6].broadcast_to((128, NEXP)), ALU.mult),
                            v.scalar_tensor_tensor(ltk[:, :], ltk[:, :], 1.0,
                                                   gsc[:, 6:7].broadcast_to((128, NEXP)),
                                                   ALU.mult, ALU.mult)))
                        step("vector", lambda v: v.tensor_tensor(
                            gtk[:, :], gtk[:, :], ltk[:, :], ALU.add))
                        step("tensor", lambda t: t.transpose(
                            ptbF[0:28, 0:128], gtk[:, :], ident[:, :]))
                        step("vector", lambda v, tcj=tcj: v.tensor_copy(
                            gfm[:, tcj * 128:(tcj + 1) * 128], ptbF[0:28, 0:128]))

                    for n in range(2):
                        step("tensor", lambda t, n=n: (
                            t.matmul(pf[3][:, :], xexpb[:, 0:128], gfm[:, n * 512:(n + 1) * 512],
                                     start=True, stop=True),
                            t.matmul(pf[4][0:96, :], xexpb[:, 128:224], gfm[:, n * 512:(n + 1) * 512],
                                     start=True, stop=True)))
                        step("vector", lambda v, n=n: (
                            v.tensor_copy(grep1[:, n * 512:(n + 1) * 512], pf[3][:, :]),
                            v.tensor_copy(grep2[:, n * 512:(n + 1) * 512], pf[4][0:96, :])))

                    for n in range(2):
                        def hmm(t, n=n):
                            for kcc in range(NKC):
                                t.matmul(pf[3][:, :], latb[:, kcc * NER:kcc * NER + 128],
                                         xam[:, xm(kcc, n * 512):xm(kcc, (n + 1) * 512)],
                                         start=(kcc == 0), stop=(kcc == NKC - 1))
                                t.matmul(pf[4][0:96, :], latb[:, kcc * NER + 128:kcc * NER + 224],
                                         xam[:, xm(kcc, n * 512):xm(kcc, (n + 1) * 512)],
                                         start=(kcc == 0), stop=(kcc == NKC - 1))
                        step("tensor", hmm)
                        step("vector", lambda v, n=n: (
                            v.tensor_tensor(htl1[:, n * 512:(n + 1) * 512], pf[3][:, :],
                                            grep1[:, n * 512:(n + 1) * 512], ALU.mult),
                            v.tensor_tensor(htl2[:, n * 512:(n + 1) * 512], pf[4][0:96, :],
                                            grep2[:, n * 512:(n + 1) * 512], ALU.mult)))

                    bw_dma = []
                    bmm_pe = []
                    y2_dve = []
                    mgs = []
                    m0 = 0
                    while m0 < NKC:
                        g_sz = min(3, NKC - m0)
                        mgs.append((m0, g_sz))
                        m0 += g_sz
                    KG = 4
                    gi = 0
                    for (m0, g_sz) in mgs:
                        for kg in range(NKC // KG):
                            i = gi * (NKC // KG) + kg

                            def bdma(g, i=i, m0=m0, g_sz=g_sz, kg=kg):
                                if i >= 2:
                                    g.wait_ge(C.sems["pe"], bmm_pe[(i - 1) * KG - 1])
                                bw_dma.append(st_w.issue(g, lambda g=g, i=i, m0=m0, g_sz=g_sz, kg=kg: g.dma_start(
                                    bwbuf[:, (i % 2) * KG * 384:(i % 2) * KG * 384 + KG * 384].rearrange(
                                        "p (k n) -> p k n", k=KG)[:, :, 0:g_sz * 128],
                                    bwt.rearrange("(kc p) n -> p kc n", p=128)[:, kg * KG:(kg + 1) * KG,
                                                                              m0 * 128:(m0 + g_sz) * 128])))
                            block.gpsimd(bdma)

                            def bmm(t, i=i, g_sz=g_sz, kg=kg, gi=gi):
                                st_w.wait(t, bw_dma[i])
                                if kg == 0:
                                    if gi == 0:
                                        C.wait(t, "dve")
                                    else:
                                        t.wait_ge(C.sems["dve"], y2_dve[gi - 1])
                                for k4 in range(KG):
                                    kcc = kg * KG + k4
                                    mm = None
                                    for m in range(g_sz):
                                        for n in range(2):
                                            mm = t.matmul(
                                                pf[m * 2 + n][:, :],
                                                bwbuf[:, (i % 2) * KG * 384 + k4 * 384 + m * 128:(i % 2) * KG * 384 + k4 * 384 + (m + 1) * 128],
                                                xam[:, xm(kcc, n * 512):xm(kcc, (n + 1) * 512)],
                                                start=(kcc == 0), stop=False)
                                    bmm_pe.append(C.inc("pe", mm))
                            block.tensor(bmm)

                        def lmm(t, m0=m0, g_sz=g_sz):
                            mm = None
                            for m in range(g_sz):
                                for n in range(2):
                                    t.matmul(pf[m * 2 + n][:, :],
                                             lbt1[:, (m0 + m) * 128:(m0 + m + 1) * 128],
                                             htl1[:, n * 512:(n + 1) * 512],
                                             start=False, stop=False)
                                    mm = t.matmul(pf[m * 2 + n][:, :],
                                                  lbt2[:, (m0 + m) * 128:(m0 + m + 1) * 128],
                                                  htl2[:, n * 512:(n + 1) * 512],
                                                  start=False, stop=True)
                            bmm_pe[-1] = C.inc("pe", mm)
                        block.tensor(lmm)

                        def y2ev(v, m0=m0, g_sz=g_sz):
                            C.wait(v, "pe")
                            for m in range(g_sz):
                                kcg = m0 + m
                                for n in range(2):
                                    v.tensor_tensor(
                                        y2b[:, kcg * TPC + n * 512:kcg * TPC + (n + 1) * 512],
                                        pf[m * 2 + n][:, :],
                                        xam[:, xm(kcg, n * 512):xm(kcg, (n + 1) * 512)],
                                        ALU.add)
                            y2_dve.append(C.inc("dve", v.engine_nop()))
                        block.vector(y2ev)
                        gi += 1

                    for kcg in range(NKC):
                        step("scalar", lambda sc, kcg=kcg: sc.activation(
                            sqsc2[:, :], y2b[:, kcg * TPC:kcg * TPC + 512], AF.Square))
                        step("tensor", lambda t, kcg=kcg: (
                            t.matmul(pf[3][0:1, :], onesb[:, 0:1], y2b[:, kcg * TPC:kcg * TPC + 512],
                                     start=(kcg == 0), stop=(kcg == NKC - 1)),
                            t.matmul(pf[4][0:1, :], onesb[:, 0:1], y2b[:, kcg * TPC + 512:(kcg + 1) * TPC],
                                     start=(kcg == 0), stop=(kcg == NKC - 1)),
                            t.matmul(pf[5][0:1, :], onesb[:, 0:1], sqsc2[:, :],
                                     start=(kcg == 0), stop=(kcg == NKC - 1))))
                        step("scalar", lambda sc, kcg=kcg: sc.activation(
                            sqsc2[:, :], y2b[:, kcg * TPC + 512:(kcg + 1) * TPC], AF.Square))
                        step("tensor", lambda t, kcg=kcg: t.matmul(
                            pf[6][0:1, :], onesb[:, 0:1], sqsc2[:, :],
                            start=(kcg == 0), stop=(kcg == NKC - 1)))

                    def rows2(v):
                        v.tensor_scalar(rowsc[0:1, 0:512], pf[3][0:1, :], 1.0 / DM, None, ALU.mult)
                        v.tensor_scalar(rowsc[0:1, 512:1024], pf[4][0:1, :], 1.0 / DM, None, ALU.mult)
                        v.tensor_scalar(rowsc2[0:1, 0:512], pf[5][0:1, :], 1.0 / DM, None, ALU.mult)
                        v.tensor_scalar(rowsc2[0:1, 512:1024], pf[6][0:1, :], 1.0 / DM, None, ALU.mult)
                    step("vector", rows2)
                    step("vector", lambda v: v.tensor_tensor(
                        xstg[0:1, 0:1024], rowsc[0:1, 0:1024], rowsc[0:1, 0:1024], ALU.mult))
                    step("vector", lambda v: v.tensor_tensor(
                        rowsc2[0:1, 0:1024], rowsc2[0:1, 0:1024], xstg[0:1, 0:1024], ALU.subtract))
                    step("vector", lambda v: v.tensor_scalar(
                        rowsc2[0:1, 0:1024], rowsc2[0:1, 0:1024], EPS, None, ALU.add))
                    step("scalar", lambda sc: sc.activation(
                        rowsc[0:1, 1024:2048], rowsc2[0:1, 0:1024], AF.Ln))
                    step("scalar", lambda sc: sc.activation(
                        rowsc[0:1, 1024:2048], rowsc[0:1, 1024:2048], AF.Exp, scale=-0.5))

                    ost = []
                    for tt in range(NCH):
                        step("tensor", lambda t, tt=tt: (
                            t.transpose(pf[3][0:128, 0:1],
                                        rowsc[0:1, tt * 128:(tt + 1) * 128], identf[0:1, 0:1]),
                            t.transpose(pf[4][0:128, 0:1],
                                        rowsc[0:1, 1024 + tt * 128:1024 + (tt + 1) * 128],
                                        identf[0:1, 0:1])))
                        step("vector", lambda v: (
                            v.tensor_copy(mcol[:, 0:1], pf[3][0:128, 0:1]),
                            v.tensor_copy(mcol[:, 1:2], pf[4][0:128, 0:1])))
                        step("scalar", lambda sc: None)

                        for qr in range(4):
                            def trs(t, tt=tt, qr=qr):
                                for k8 in range(8):
                                    kcg = qr * 8 + k8
                                    t.transpose(ptbF[:, k8 * 128:(k8 + 1) * 128],
                                                y2b[:, kcg * TPC + tt * 128:kcg * TPC + (tt + 1) * 128],
                                                ident[:, :])
                            step("tensor", trs)

                            def nrm(v, tt=tt, qr=qr):
                                if len(ost) >= 2:
                                    st_o.wait(v, ost[-2])
                                for k8 in range(8):
                                    v.tensor_scalar(
                                        xstg[:, (qr % 2) * 1024 + k8 * 128:(qr % 2) * 1024 + (k8 + 1) * 128],
                                        ptbF[:, k8 * 128:(k8 + 1) * 128],
                                        mcol[:, 0:1], mcol[:, 1:2],
                                        ALU.subtract, ALU.mult)
                            step("vector", nrm)

                            def outdma(s, tt=tt, qr=qr):
                                for nm in CSEMS:
                                    C.wait(s, nm)
                                ost.append(st_o.issue(s, lambda s=s, tt=tt, qr=qr: s.dma_start(
                                    out[tt * 128:(tt + 1) * 128, qr * 1024:(qr + 1) * 1024],
                                    xstg[:, (qr % 2) * 1024:(qr % 2 + 1) * 1024])))
                            block.sync(outdma)

                    end_block(block)
        else:
            with contextlib.ExitStack() as esZ:
                sbZ, _ = mk(esZ)
                zb = sbZ("zb", [128, 2048], F32)
                with nc.Block() as block:
                    cur_block[0] = block
                    step("vector", lambda v: v.memset(zb[:, :], 0))

                    def zdma(s):
                        C.wait(s, "dve")
                        for tt in range(NCH):
                            for half in range(2):
                                st_o.issue(s, lambda s=s, tt=tt, half=half: s.dma_start(
                                    out[tt * 128:(tt + 1) * 128, half * 2048:(half + 1) * 2048],
                                    zb[:, :]))
                        st_o.wait_all(s)
                    block.sync(zdma)
                    end_block(block)

    return nc, tap_d



def _host_constants(core):
    bf = ml_dtypes.bfloat16
    ident = np.eye(128, dtype=np.float32).astype(bf)
    renc = np.zeros((128, 128), dtype=np.float32)
    for dp in range(64):
        renc[dp + 64, dp] = -1.0
        renc[dp, dp + 64] = 1.0
    renc = renc.astype(bf)
    start = core * TPC - HALO
    pos = np.arange(start, start + XT, dtype=np.float64)
    inv = 1.0 / (10000.0 ** (2.0 * np.arange(64, dtype=np.float64) / 128.0))
    fr = np.concatenate([inv, inv])[:, None] * pos[None, :]
    cost = np.cos(fr).astype(np.float32).astype(bf)
    sint = np.sin(fr).astype(np.float32).astype(bf)
    i = np.arange(128)[:, None]
    j = np.arange(WIN)[None, :]
    m = ((j >= i) & (j <= i + 64)).astype(np.float32)
    mf = m * (j >= 64)
    if core == 0:
        maskc = np.concatenate([m, mf], axis=1).astype(bf)
    else:
        maskc = np.concatenate([m, m], axis=1).astype(bf)
    it = np.arange(128)[None, :]
    je = np.arange(128)[:, None]
    t1 = 0.5 * ((je >= 64) & (je - 64 <= it)).astype(np.float32)
    t2 = 0.5 * ((je + 64 <= it)).astype(np.float32)
    trimv = np.concatenate([t1, t2], axis=1).astype(bf)
    ones = np.zeros((128, 2), dtype=np.float32)
    ones[:, 0] = 1.0
    onesr = np.ones((1, 128), dtype=np.float32).astype(bf)
    onesbf = np.ones((128, 1), dtype=np.float32)
    onesrf = np.ones((1, 128), dtype=np.float32)
    cm = np.zeros((128, 16), dtype=np.float32)
    cm[:, 0:8] = (np.arange(8)[None, :] < core).astype(np.float32)
    cm[:, 8:16] = 1.0 - cm[:, 0:8]
    xexp = np.zeros((NEXP, NER), dtype=np.float32)
    for e in range(NEXP):
        xexp[e, e * 8:(e + 1) * 8] = 2.0
    return dict(identh=ident, identfh=np.eye(128, dtype=np.float32), rench=renc,
                cosh=cost, sinh=sint, maskh=maskc, trih=trimv,
                onesh=ones.astype(bf), onesrh=onesr, onesbfh=onesbf,
                onesrfh=onesrf, cmaskh=cm, xexph=xexp.astype(bf))


def _host_inputs(inputs):
    x = np.asarray(inputs["L0_attn_hidden_states"]).reshape(NTOK, DM)
    xpad = np.concatenate([np.zeros((HALO, DM), np.float32), x], axis=0)
    base = {
        "wq": np.ascontiguousarray(np.asarray(inputs["L0_attn_Wq"])),
        "wk": np.ascontiguousarray(np.asarray(inputs["L0_attn_Wk"])),
        "wv": np.ascontiguousarray(np.asarray(inputs["L0_attn_Wv"])),
        "wo": np.ascontiguousarray(np.asarray(inputs["L0_attn_Wo"])),
        "bwt": np.ascontiguousarray(np.asarray(inputs["L0_ffn_base_W"]).T),
        "gwt": np.ascontiguousarray(np.asarray(inputs["L0_ffn_gate_W"]).T),
        "lat": np.ascontiguousarray(
            np.asarray(inputs["L0_ffn_lora_A"]).transpose(2, 0, 1).reshape(DM, NER)),
        "lbt": np.ascontiguousarray(
            np.asarray(inputs["L0_ffn_lora_B"]).transpose(0, 2, 1).reshape(NER, DM)),
    }
    maps = []
    for c in range(NCORE):
        m = dict(base)
        m["x"] = np.ascontiguousarray(xpad[c * TPC:c * TPC + XT])
        m.update(_host_constants(c))
        maps.append(m)
    return maps


def kernel(**inputs):
    maps = _host_inputs(inputs)
    nc, _ = build_program()
    res = run_bass_kernel_spmd(nc, maps, core_ids=list(range(NCORE)))
    outs = [res.results[c]["out"] for c in range(NCORE)]
    return np.concatenate(outs, axis=0).reshape(1, NTOK, DM)
